# version 09
# baseline (speedup 1.0000x reference)
import numpy as np
import ml_dtypes
from contextlib import ExitStack

import concourse.bass as bass
import concourse.mybir as mybir
from concourse.bass_utils import run_bass_kernel_spmd

F32 = mybir.dt.float32
BF16 = mybir.dt.bfloat16
AF = mybir.ActivationFunctionType
ALU = mybir.AluOpType
AX = mybir.AxisListType

NCORES = 8
B = 32
T = 2048
E = 1024
H = 16
D = 64
SLOC = T // NCORES
FF = 2048
NEXP = 8
EPS = 1e-5
BFNP = np.dtype(ml_dtypes.bfloat16)


class Ch:

    def __init__(self, nc, name):
        self.sem = nc.alloc_semaphore(name)
        self.n = 0

    def inc(self, instr, k=1):
        instr.then_inc(self.sem, k)
        self.n += k
        return instr

    def dma(self, instr):
        return self.inc(instr, 16)

    def pre(self, eng):
        eng.wait_ge(self.sem, self.n)

    def wait(self, eng):
        eng.wait_ge(self.sem, self.n)


class SE:

    def __init__(self, nc, raw, name):
        object.__setattr__(self, "raw", raw)
        object.__setattr__(self, "sem", nc.alloc_semaphore(name))
        object.__setattr__(self, "n", 0)

    def __getattr__(self, opname):
        raw_fn = getattr(self.raw, opname)

        def emit(*a, **kw):
            self.raw.wait_ge(self.sem, self.n)
            ins = raw_fn(*a, **kw)
            ins.then_inc(self.sem, 1)
            object.__setattr__(self, "n", self.n + 1)
            return ins
        return emit

    def wait(self, other_raw_eng):
        other_raw_eng.wait_ge(self.sem, self.n)


def consts_np():
    ident = np.eye(128, dtype=BFNP)
    c64 = np.eye(64, dtype=np.float32) - np.full((64, 64), 1.0 / 64,
                                                 dtype=np.float32)
    cblk = np.zeros((128, 128), dtype=np.float32)
    cblk[0:64, 0:64] = c64
    cblk[64:128, 64:128] = c64
    onesb = np.zeros((128, 2), dtype=np.float32)
    onesb[0:64, 0] = 1.0
    onesb[64:128, 1] = 1.0
    return ident, cblk.astype(BFNP), onesb.astype(BFNP)


def ln_stats(nc, V, A, x_f32, mu, r, work, junk):
    V.wait(nc.scalar)
    V.tensor_reduce(out=work[:, 0:1], in_=x_f32, axis=AX.X, op=ALU.add)
    A.activation(out=junk[:, :], in_=x_f32, func=AF.Square,
                 accum_out=work[:, 1:2])
    A.wait(nc.vector)
    V.tensor_scalar(out=mu[:, 0:1], in0=work[:, 0:1], scalar1=1.0 / E,
                    scalar2=None, op0=ALU.mult)
    V.tensor_scalar(out=work[:, 2:3], in0=work[:, 1:2], scalar1=1.0 / E,
                    scalar2=None, op0=ALU.mult)
    V.tensor_tensor(out=work[:, 3:4], in0=mu[:, 0:1], in1=mu[:, 0:1],
                    op=ALU.mult)
    V.tensor_tensor(out=work[:, 4:5], in0=work[:, 2:3], in1=work[:, 3:4],
                    op=ALU.subtract)
    V.tensor_scalar(out=work[:, 5:6], in0=work[:, 4:5], scalar1=EPS,
                    scalar2=None, op0=ALU.add)
    V.wait(nc.scalar)
    A.activation(out=work[:, 6:7], in_=work[:, 5:6], func=AF.Sqrt)
    A.wait(nc.vector)
    V.reciprocal(out=r[:, 0:1], in_=work[:, 6:7])


def build_l1(n_tiles=None):
    import os
    if n_tiles is None:
        n_tiles = int(os.environ.get("L1_TILES", "999"))
    stage = int(os.environ.get("L1_STAGE", "99"))
    nc = bass.Bass("TRN2", target_bir_lowering=False)
    x_in = nc.dram_tensor("x", [B, SLOC, E], F32, kind="ExternalInput")
    wk_in = nc.dram_tensor("wk", [E, E], F32, kind="ExternalInput")
    wv_in = nc.dram_tensor("wv", [E, E], F32, kind="ExternalInput")
    kct_out = nc.dram_tensor("kct", [B, H, 65, SLOC], BF16,
                             kind="ExternalOutput")
    v_out = nc.dram_tensor("vv", [B, H, SLOC, D], BF16, kind="ExternalOutput")

    ident_np, cblk_np, onesb_np = consts_np()
    ident_dr = nc.inline_tensor(ident_np, name="identc")
    cblk_dr = nc.inline_tensor(cblk_np, name="cblkc")
    onesb_dr = nc.inline_tensor(onesb_np, name="onesbc")

    ident = nc.alloc_sbuf_tensor("identsb", [128, 128], BF16)
    cblk = nc.alloc_sbuf_tensor("cblksb", [128, 128], BF16)
    onesb = nc.alloc_sbuf_tensor("onesbsb", [128, 2], BF16)
    wkn = nc.alloc_sbuf_tensor("wkn", [128, 8, E], BF16)
    wkT = nc.alloc_sbuf_tensor("wkT", [128, 8, E], BF16)
    wkc = nc.alloc_sbuf_tensor("wkc", [128, 8, E], BF16)
    wv = nc.alloc_sbuf_tensor("wvsb", [128, 8, E], BF16)
    xt = nc.alloc_sbuf_tensor("xt", [128, E], F32)
    mu = nc.alloc_sbuf_tensor("mu", [128, 1], F32)
    r = nc.alloc_sbuf_tensor("r", [128, 1], F32)
    work = nc.alloc_sbuf_tensor("work", [128, 8], F32)
    junk = nc.alloc_sbuf_tensor("junk", [128, E], BF16)
    xn32 = nc.alloc_sbuf_tensor("xn32", [128, E], F32)
    xnbf = nc.alloc_sbuf_tensor("xnbf", [128, E], BF16)
    xnT = nc.alloc_sbuf_tensor("xnT", [128, 8, 128], BF16)
    kcbf = nc.alloc_sbuf_tensor("kcbf", [128, 8, 128], BF16)
    sqbf = nc.alloc_sbuf_tensor("sqbf", [128, 8, 128], BF16)
    kvarT = nc.alloc_sbuf_tensor("kvarT", [2, 8, 128], BF16)
    vbf = nc.alloc_sbuf_tensor("vbf", [128, E], BF16)
    ps_t = nc.alloc_psum_tensor("ps_t", [128, 8, 128], BF16)
    ps_k = nc.alloc_psum_tensor("ps_k", [128, 8, 128], F32)
    ps_v = nc.alloc_psum_tensor("ps_v", [128, E], F32)
    ps_w = nc.alloc_psum_tensor("ps_w", [2, 8, 128], F32)

    ch_in = Ch(nc, "ch_in")
    ch_ing = Ch(nc, "ch_ing")
    ch_pe = Ch(nc, "ch_pe")
    ch_out = Ch(nc, "ch_out")

    with ExitStack() as st:
        block = st.enter_context(nc.Block())

        @block.sync
        def _(sync):
            V = SE(nc, nc.vector, "se_dve")
            A = SE(nc, nc.scalar, "se_act")

            ch_in.dma(nc.sync.dma_start(out=ident[:, :], in_=ident_dr.ap()))
            ch_in.pre(nc.sync)
            ch_in.dma(nc.sync.dma_start(out=cblk[:, :], in_=cblk_dr.ap()))
            ch_in.pre(nc.sync)
            ch_in.dma(nc.sync.dma_start(out=onesb[:, :], in_=onesb_dr.ap()))
            ch_ing.dma(nc.gpsimd.dma_start(
                out=wkn[:, :, :],
                in_=wk_in.ap().rearrange("(c p) d -> p c d", p=128)))
            ch_ing.pre(nc.gpsimd)
            ch_ing.dma(nc.gpsimd.dma_start(
                out=wv[:, :, :],
                in_=wv_in.ap().rearrange("(c p) d -> p c d", p=128)))
            ch_in.wait(nc.tensor)
            ch_ing.wait(nc.tensor)
            for cj in range(8):
                for ci in range(8):
                    ins = nc.tensor.transpose(
                        ps_t[:, ci, :], wkn[:, ci, 128 * cj:128 * (cj + 1)],
                        ident[:, :])
                ch_pe.inc(ins)
                ch_pe.wait(nc.vector)
                V.tensor_copy(out=wkT[:, cj, :],
                              in_=ps_t[:, :, :].rearrange("p c f -> p (c f)"))
                V.wait(nc.tensor)
            for ec in range(8):
                for g in range(8):
                    ins = nc.tensor.matmul(
                        ps_k[:, g, :], wkT[:, g, 128 * ec:128 * (ec + 1)],
                        cblk[:, :], start=True, stop=True)
                ch_pe.inc(ins)
                ch_pe.wait(nc.vector)
                V.tensor_copy(out=wkc[:, ec, :],
                              in_=ps_k[:, :, :].rearrange("p c f -> p (c f)"))
                V.wait(nc.tensor)

            tcount = 0
            for b_o in range(B):
                for sc in range(SLOC // 128):
                    if tcount >= n_tiles:
                        continue
                    tcount += 1
                    s0 = 128 * sc
                    V.wait(nc.sync)
                    A.wait(nc.sync)
                    ch_out.wait(nc.vector)
                    ch_in.pre(nc.sync)
                    ch_in.dma(nc.sync.dma_start(
                        out=xt[:, :], in_=x_in.ap()[b_o, s0:s0 + 128, :]))
                    ch_in.wait(nc.vector)
                    ch_in.wait(nc.scalar)
                    if stage < 1:
                        continue
                    ln_stats(nc, V, A, xt[:, :], mu, r, work, junk)
                    V.tensor_scalar(out=xn32[:, :], in0=xt[:, :],
                                    scalar1=mu[:, 0:1], scalar2=r[:, 0:1],
                                    op0=ALU.subtract, op1=ALU.mult)
                    V.tensor_copy(out=xnbf[:, :], in_=xn32[:, :])
                    V.wait(nc.tensor)
                    if stage < 2:
                        continue
                    for c in range(8):
                        ins = nc.tensor.transpose(
                            ps_t[:, c, :], xnbf[:, 128 * c:128 * (c + 1)],
                            ident[:, :])
                    ch_pe.inc(ins)
                    ch_pe.wait(nc.vector)
                    V.tensor_copy(out=xnT[:, :, :], in_=ps_t[:, :, :])
                    V.wait(nc.tensor)
                    if stage < 3:
                        continue
                    for dc in range(8):
                        for ec in range(8):
                            ins = nc.tensor.matmul(
                                ps_k[:, dc, :],
                                wkc[:, ec, 128 * dc:128 * (dc + 1)],
                                xnT[:, ec, :],
                                start=(ec == 0), stop=(ec == 7))
                        ch_pe.inc(ins)
                    ch_pe.wait(nc.scalar)
                    ch_pe.wait(nc.vector)
                    V.tensor_copy(out=kcbf[:, 0:4, :], in_=ps_k[:, 0:4, :])
                    ndve0 = V.n
                    V.tensor_copy(out=kcbf[:, 4:8, :], in_=ps_k[:, 4:8, :])
                    ndve1 = V.n
                    nc.scalar.wait_ge(V.sem, ndve0)
                    A.activation(out=sqbf[:, 0:4, :], in_=ps_k[:, 0:4, :],
                                 func=AF.Square)
                    nc.scalar.wait_ge(V.sem, ndve1)
                    A.activation(out=sqbf[:, 4:8, :], in_=ps_k[:, 4:8, :],
                                 func=AF.Square)
                    V.wait(nc.tensor)
                    A.wait(nc.tensor)
                    if stage < 4:
                        continue
                    for c in range(8):
                        ins = nc.tensor.matmul(ps_w[:, c, :], onesb[:, :],
                                               sqbf[:, c, :],
                                               start=True, stop=True)
                    ch_pe.inc(ins)
                    ch_pe.wait(nc.vector)
                    V.tensor_scalar(out=kvarT[:, :, :], in0=ps_w[:, :, :],
                                    scalar1=1.0 / 63.0, scalar2=None,
                                    op0=ALU.mult)
                    if stage < 5:
                        continue
                    for nh in range(2):
                        for ec in range(8):
                            ins = nc.tensor.matmul(
                                ps_v[:, 512 * nh:512 * (nh + 1)],
                                xnT[:, ec, :],
                                wv[:, ec, 512 * nh:512 * (nh + 1)],
                                start=(ec == 0), stop=(ec == 7))
                        ch_pe.inc(ins)
                    ch_pe.wait(nc.vector)
                    V.tensor_copy(out=vbf[:, :], in_=ps_v[:, :])
                    if stage < 6:
                        continue
                    V.wait(nc.sync)
                    kv_view = kct_out.ap()[b_o]
                    for c in range(8):
                        for j in range(2):
                            dst = kv_view[2 * c + j, 0:64, s0:s0 + 128]
                            ch_out.pre(nc.sync)
                            ch_out.dma(nc.sync.dma_start(
                                out=dst, in_=kcbf[64 * j:64 * j + 64, c, :]))
                    dstv = kv_view[:, 64, s0:s0 + 128].rearrange(
                        "(c p) t -> p c t", p=2)
                    ch_out.pre(nc.sync)
                    ch_out.dma(nc.sync.dma_start(out=dstv, in_=kvarT[:, :, :]))
                    for hh in range(2):
                        dst = v_out.ap()[b_o, 8 * hh:8 * hh + 8,
                                         s0:s0 + 128, :]
                        ch_out.pre(nc.sync)
                        ch_out.dma(nc.sync.dma_start(
                            out=dst.rearrange("h t d -> t h d"),
                            in_=vbf[:, :].rearrange("t (h d) -> t h d", d=D)
                            [:, 8 * hh:8 * hh + 8, :]))
            ch_out.wait(nc.sync)
    return nc


def run_l1(x, wk, wv):
    nc = build_l1()
    in_maps = [{
        "x": np.ascontiguousarray(x[:, SLOC * c:SLOC * (c + 1), :]),
        "wk": wk, "wv": wv,
    } for c in range(NCORES)]
    res = run_bass_kernel_spmd(nc, in_maps, core_ids=list(range(NCORES)))
    kct = np.concatenate([res.results[c]["kct"] for c in range(NCORES)],
                         axis=3)
    vv = np.concatenate([res.results[c]["vv"] for c in range(NCORES)], axis=2)
    return kct, vv


def make_in_maps(x, w):
    in_maps = []
    for c in range(NCORES):
        in_maps.append({
            "x": np.ascontiguousarray(x[:, SLOC * c:SLOC * (c + 1), :]),
            "xh": np.ascontiguousarray(
                x[:, (T // 2) * (c % 2):(T // 2) * (c % 2 + 1), :]),
            "wk": w["k"], "wv": w["v"],
            "wq": w["q"], "wo": w["o"], "ipw": w["ipw"], "ipb": w["ipb"],
            "l1w": w["l1w"], "l1b": w["l1b"], "opw": w["opw"],
            "opb": w["opb"], "l2w": w["l2w"], "l2b": w["l2b"],
            "rw": w["rw"], "w1": w["w1"], "w2": w["w2"],
        })
    return in_maps


def weights_of(inputs):
    f = lambda k: np.ascontiguousarray(np.asarray(inputs[k], dtype=np.float32))
    return f("L0_attn_x"), {
        "k": f("L0_attn_Wk"), "v": f("L0_attn_Wv"),
        "q": f("L0_attn_Wq"), "o": f("L0_attn_Wo"),
        "ipw": f("L0_ffn_in_proj_w"), "ipb": f("L0_ffn_in_proj_b"),
        "l1w": f("L0_ffn_ln1_w"), "l1b": f("L0_ffn_ln1_b"),
        "opw": f("L0_ffn_out_proj_w"), "opb": f("L0_ffn_out_proj_b"),
        "l2w": f("L0_ffn_ln2_w"), "l2b": f("L0_ffn_ln2_b"),
        "rw": f("L0_ffn_router_w"), "w1": f("L0_ffn_w1"),
        "w2": f("L0_ffn_w2")}


def kernel(**inputs):
    x, w = weights_of(inputs)
    nc = build_l2(stage=99)
    res = run_bass_kernel_spmd(nc, make_in_maps(x, w),
                               core_ids=list(range(NCORES)))
    return np.concatenate([res.results[c]["out"] for c in range(NCORES)],
                          axis=1)


def build_l2(stage=99):
    import os
    nc = bass.Bass("TRN2", target_bir_lowering=False,
                   detect_race_conditions=(os.environ.get("NORACE") != "1"))
    x_in = nc.dram_tensor("x", [B, SLOC, E], F32, kind="ExternalInput")
    x2_in = nc.dram_tensor("xh", [B, T // 2, E], F32, kind="ExternalInput")
    wk_in = nc.dram_tensor("wk", [E, E], F32, kind="ExternalInput")
    wv_in = nc.dram_tensor("wv", [E, E], F32, kind="ExternalInput")
    kct_sh = nc.dram_tensor("KCTsh", [B, H, 65, T], BF16, kind="Internal",
                            addr_space="Shared")
    v_sh = nc.dram_tensor("Vsh", [B, H, T, D], BF16, kind="Internal",
                          addr_space="Shared")
    flag_sh = nc.dram_tensor("FLAGsh", [2, 16], mybir.dt.uint32,
                             kind="Internal", addr_space="Shared")
    kct_pr = nc.dram_tensor("KCTpr", [B, H, 65, T // 2], BF16)
    v_pr = nc.dram_tensor("Vpr", [B, H, T // 2, D], BF16)
    wq_in = nc.dram_tensor("wq", [E, E], F32, kind="ExternalInput")
    wo_in = nc.dram_tensor("wo", [E, E], F32, kind="ExternalInput")
    ipw_in = nc.dram_tensor("ipw", [3 * E, E], F32, kind="ExternalInput")
    ipb_in = nc.dram_tensor("ipb", [3 * E], F32, kind="ExternalInput")
    l1w_in = nc.dram_tensor("l1w", [E], F32, kind="ExternalInput")
    l1b_in = nc.dram_tensor("l1b", [E], F32, kind="ExternalInput")
    opw_in = nc.dram_tensor("opw", [E, E], F32, kind="ExternalInput")
    opb_in = nc.dram_tensor("opb", [E], F32, kind="ExternalInput")
    l2w_in = nc.dram_tensor("l2w", [E], F32, kind="ExternalInput")
    l2b_in = nc.dram_tensor("l2b", [E], F32, kind="ExternalInput")
    rw_in = nc.dram_tensor("rw", [NEXP, E], F32, kind="ExternalInput")
    w1_in = nc.dram_tensor("w1", [NEXP, E, FF], F32, kind="ExternalInput")
    w2_in = nc.dram_tensor("w2", [NEXP, FF, E], F32, kind="ExternalInput")
    out_dr = nc.dram_tensor("out", [B, SLOC, E], F32, kind="ExternalOutput")

    qt_dr = nc.dram_tensor("QT", [B, H, 65, SLOC], BF16)
    qkv_dr = nc.dram_tensor("QKV", [SLOC, B, 3, H, D], BF16)
    xn_dr = nc.dram_tensor("XNs", [SLOC, B, E], F32)
    x1_dr = nc.dram_tensor("X1s", [SLOC, B, E], F32)
    hn2_dr = nc.dram_tensor("HN2T", [16, 8, 128, 512], BF16)
    y_dr = nc.dram_tensor("Ys", [SLOC * B, E], F32)

    ident_np, cblk_np, onesb_np = consts_np()
    identf_np = np.eye(128, dtype=np.float32)
    mask_np = np.zeros((128, H, 32), dtype=np.float32)
    for p in range(128):
        mask_np[p, :, 0:(p % 32) + 1] = 1.0
    ident_dr = nc.inline_tensor(ident_np, name="identc")
    identf_dr = nc.inline_tensor(identf_np, name="identfc")
    onesb_dr = nc.inline_tensor(onesb_np, name="onesbc")
    cblk_dr = nc.inline_tensor(cblk_np, name="cblkc")
    mask_dr = nc.inline_tensor(mask_np.astype(BFNP), name="maskc")

    ident = nc.alloc_sbuf_tensor("identsb", [128, 128], BF16)
    identf = nc.alloc_sbuf_tensor("identfsb", [128, 128], F32)
    onesb = nc.alloc_sbuf_tensor("onesbsb", [128, 2], BF16)
    mask = nc.alloc_sbuf_tensor("masksb", [128, H, 32], BF16)
    gates = nc.alloc_sbuf_tensor("gatessb", [128, 64 * NEXP], F32)
    ln2w = nc.alloc_sbuf_tensor("ln2w", [128, 8], F32)
    ln2b = nc.alloc_sbuf_tensor("ln2b", [128, 8], F32)
    ps_A = nc.alloc_psum_tensor("ps_A", [128, 1024], F32)
    ps_B = nc.alloc_psum_tensor("ps_B", [128, 1024], F32)
    ps_C = nc.alloc_psum_tensor("ps_C", [128, 1024], F32)
    ps_D = nc.alloc_psum_tensor("ps_D", [128, 512], F32)
    ps_E = nc.alloc_psum_tensor("ps_E", [128, 512], F32)

    ch_in = Ch(nc, "ch_in")
    ch_ing = Ch(nc, "ch_ing")
    ch_pe = Ch(nc, "ch_pe")
    ch_st = Ch(nc, "ch_st")
    ch_res = Ch(nc, "ch_res")

    ALLCH = [ch_in, ch_ing, ch_pe, ch_st, ch_res]

    with ExitStack() as st:
        block = st.enter_context(nc.Block())

        @block.sync
        def _(sync):
            V = SE(nc, nc.vector, "se_dve")
            A = SE(nc, nc.scalar, "se_act")

            def fence():
                for eng in (nc.tensor, nc.vector, nc.scalar, nc.sync,
                            nc.gpsimd):
                    for ch in ALLCH:
                        ch.wait(eng)
                    V.wait(eng)
                    A.wait(eng)

            def ld_sync(dst, src):
                ch_in.pre(nc.sync)
                ch_in.dma(nc.sync.dma_start(out=dst, in_=src))

            def ld_g(dst, src):
                ch_ing.pre(nc.gpsimd)
                ch_ing.dma(nc.gpsimd.dma_start(out=dst, in_=src))

            def st_sync(dst, src):
                ch_st.pre(nc.sync)
                ch_st.dma(nc.sync.dma_start(out=dst, in_=src))

            def transpose_f32(dst_bf, src_f32, n, pst):
                V.wait(nc.tensor)
                A.wait(nc.tensor)
                for c in range(n):
                    ins = nc.tensor.transpose(
                        pst[:, 128 * c:128 * (c + 1)],
                        src_f32[:, 128 * c:128 * (c + 1)], identf[:, :])
                ch_pe.inc(ins)
                ch_pe.wait(nc.vector)
                V.tensor_copy(
                    out=dst_bf[:, 0:n, :],
                    in_=pst[:, 0:128 * n].rearrange("p (c f) -> p c f", f=128))
                V.wait(nc.tensor)

            ld_sync(ident[:, :], ident_dr.ap())
            ld_sync(identf[:, :], identf_dr.ap())
            ld_sync(onesb[:, :], onesb_dr.ap())
            ld_sync(mask[:, :, :], mask_dr.ap())

            vtmp8 = nc.alloc_sbuf_tensor("vtmp8", [8, 128], F32)

            def load_col(dst_col, vec_ap):
                ch_pe.wait(nc.sync)
                ld_sync(vtmp8[:, :], vec_ap.rearrange("(c p) -> c p", p=128))
                ch_in.wait(nc.tensor)
                V.wait(nc.tensor)
                ins = nc.tensor.transpose(ps_E[:, 0:8], vtmp8[:, :],
                                          identf[0:8, 0:8])
                ch_pe.inc(ins)
                ch_pe.wait(nc.vector)
                V.tensor_copy(out=dst_col, in_=ps_E[:, 0:8])
                V.wait(nc.tensor)

            load_col(ln2w[:, :], l2w_in.ap())
            load_col(ln2b[:, :], l2b_in.ap())

            ipbrow = nc.alloc_sbuf_tensor("ipbrow", [1, 3 * E], F32)
            fbz = nc.alloc_sbuf_tensor("fbz", [2, 16], mybir.dt.uint32)
            fone = nc.alloc_sbuf_tensor("fone", [1, 16], mybir.dt.uint32)
            tmp32 = nc.alloc_sbuf_tensor("tmp32", [128, E], F32)
            xt = nc.alloc_sbuf_tensor("xt", [128, E], F32)
            mu = nc.alloc_sbuf_tensor("mu", [128, 1], F32)
            rr = nc.alloc_sbuf_tensor("rr", [128, 1], F32)
            r2 = nc.alloc_sbuf_tensor("r2", [128, 1], F32)
            work = nc.alloc_sbuf_tensor("work", [128, 16], F32)
            junk = nc.alloc_sbuf_tensor("junk", [128, E], BF16)
            ph1 = ExitStack()
            wq = ph1.enter_context(nc.sbuf_tensor("wqsb", [128, 8, E], BF16))
            ipT = ph1.enter_context(nc.sbuf_tensor("ipT", [128, 8, 3 * E], BF16))
            brep = ph1.enter_context(nc.sbuf_tensor("brep", [128, 3 * E], F32))
            ln1w = ph1.enter_context(nc.sbuf_tensor("ln1w", [128, 8], F32))
            ln1bb = ph1.enter_context(nc.sbuf_tensor("ln1bb", [128, 8], BF16))
            xn32 = ph1.enter_context(nc.sbuf_tensor("xn32", [128, E], F32))
            xnT = ph1.enter_context(nc.sbuf_tensor("xnT", [128, 8, 128], BF16))
            qtb = ph1.enter_context(nc.sbuf_tensor("qtb", [128, 8, 128], BF16))
            sqb = ph1.enter_context(nc.sbuf_tensor("sqb", [128, 8, 128], BF16))
            stq = ph1.enter_context(nc.sbuf_tensor("stq", [2, 8, 128], F32))
            sts = ph1.enter_context(nc.sbuf_tensor("sts", [2, 8, 128], F32))
            qvarT = ph1.enter_context(nc.sbuf_tensor("qvarT", [2, 8, 128], BF16))
            st3 = ph1.enter_context(nc.sbuf_tensor("st3", [2, 8, 128], F32))
            qkvb = ph1.enter_context(nc.sbuf_tensor("qkvb", [128, 3 * E], BF16))

            ld_g(wq[:, :, :], wq_in.ap().rearrange("(c p) d -> p c d", p=128))
            load_col(ln1w[:, :], l1w_in.ap())
            for i in range(24):
                ch_pe.wait(nc.sync)
                ld_sync(tmp32[:, :], ipw_in.ap()[128 * i:128 * (i + 1), :])
                ch_in.wait(nc.tensor)
                V.wait(nc.tensor)
                for j in range(8):
                    ins = nc.tensor.transpose(
                        ps_C[:, 512 * (j % 2):512 * (j % 2) + 128],
                        tmp32[:, 128 * j:128 * (j + 1)], identf[:, :])
                    ch_pe.inc(ins)
                    ch_pe.wait(nc.vector)
                    V.tensor_copy(out=ipT[:, j, 128 * i:128 * (i + 1)],
                                  in_=ps_C[:, 512 * (j % 2):512 * (j % 2) + 128])
                    V.wait(nc.tensor)
            ln1b_tmp = ph1.enter_context(nc.sbuf_tensor("ln1btmp", [128, 8], F32))
            load_col(ln1b_tmp[:, :], l1b_in.ap())
            V.tensor_copy(out=ln1bb[:, :], in_=ln1b_tmp[:, :])
            V.wait(nc.tensor)
            for n6 in range(6):
                for ec in range(8):
                    ins = nc.tensor.matmul(
                        ps_D[0:1, 0:512], ln1bb[:, ec:ec + 1],
                        ipT[:, ec, 512 * n6:512 * (n6 + 1)],
                        start=(ec == 0), stop=(ec == 7))
                ch_pe.inc(ins)
                ch_pe.wait(nc.vector)
                V.tensor_copy(out=brep[0:1, 512 * n6:512 * (n6 + 1)],
                              in_=ps_D[0:1, 0:512])
                V.wait(nc.tensor)
            ld_sync(ipbrow[:, :], ipb_in.ap()[None, :])
            ch_in.wait(nc.vector)
            V.tensor_tensor(out=brep[0:1, :], in0=brep[0:1, :],
                            in1=ipbrow[0:1, :], op=ALU.add)
            V.wait(nc.sync)
            rep = 1
            while rep < 128:
                ch_st.pre(nc.sync)
                ch_st.dma(nc.sync.dma_start(
                    out=brep[rep:2 * rep, :], in_=brep[0:rep, :]))
                rep *= 2
            ch_st.wait(nc.vector)
            for j in range(8):
                V.tensor_scalar(out=ipT[:, j, :], in0=ipT[:, j, :],
                                scalar1=ln1w[:, j:j + 1], scalar2=None,
                                op0=ALU.mult)


            V.memset(fbz[:, :], 0)
            V.memset(fone[:, :], 1)
            ch_stg = Ch(nc, "ch_stg")
            ch_sts = Ch(nc, "ch_sts")
            ALLCH.append(ch_sts)
            nullsem = nc.alloc_semaphore("nullsem")
            nullsem2 = nc.alloc_semaphore("nullsem2")
            import os as _os

            ph0 = ExitStack()
            wkc0 = ph0.enter_context(nc.sbuf_tensor("wkc0", [128, 8, E], BF16))
            wv0 = ph0.enter_context(nc.sbuf_tensor("wv0", [128, 8, E], BF16))
            wkT0 = ph0.enter_context(nc.sbuf_tensor("wkT0", [128, 8, E], BF16))
            cblk0 = ph0.enter_context(nc.sbuf_tensor("cblk0", [128, 128], BF16))
            xn32k = ph0.enter_context(nc.sbuf_tensor("xn32k", [128, E], F32))
            xnTk = ph0.enter_context(nc.sbuf_tensor("xnTk", [128, 8, 128], BF16))
            kcb0 = ph0.enter_context(nc.sbuf_tensor("kcb0", [128, 8, 128], BF16))
            sqb0 = ph0.enter_context(nc.sbuf_tensor("sqb0", [128, 8, 128], BF16))
            kvarT0 = ph0.enter_context(nc.sbuf_tensor("kvarT0", [2, 8, 128], BF16))
            vbf0 = ph0.enter_context(nc.sbuf_tensor("vbf0", [128, E], BF16))
            ld_sync(cblk0[:, :], cblk_dr.ap())
            ld_g(wv0[:, :, :], wv_in.ap().rearrange("(c p) d -> p c d", p=128))
            for i in range(8):
                ch_pe.wait(nc.sync)
                ld_sync(tmp32[:, :], wk_in.ap()[128 * i:128 * (i + 1), :])
                ch_in.wait(nc.tensor)
                V.wait(nc.tensor)
                for j in range(8):
                    ins = nc.tensor.transpose(
                        ps_C[:, 512 * (j % 2):512 * (j % 2) + 128],
                        tmp32[:, 128 * j:128 * (j + 1)], identf[:, :])
                    ch_pe.inc(ins)
                    ch_pe.wait(nc.vector)
                    V.tensor_copy(out=wkT0[:, j, 128 * i:128 * (i + 1)],
                                  in_=ps_C[:, 512 * (j % 2):512 * (j % 2) + 128])
                    V.wait(nc.tensor)
            ch_in.wait(nc.tensor)
            for ec in range(8):
                for g8 in range(8):
                    ins = nc.tensor.matmul(
                        ps_A[:, 128 * g8:128 * (g8 + 1)],
                        wkT0[:, g8, 128 * ec:128 * (ec + 1)],
                        cblk0[:, :], start=True, stop=True)
                ch_pe.inc(ins)
                ch_pe.wait(nc.vector)
                V.tensor_copy(out=wkc0[:, ec, :],
                              in_=ps_A[:, :].rearrange("p (c f) -> p c f",
                                                       f=128))
                V.wait(nc.tensor)
            ch_ing.wait(nc.tensor)
            _kvpart = _os.environ.get("KVPART", "")
            if _os.environ.get("DSPROBE") == "2":
                ch_stg.dma(nc.gpsimd.dma_start(
                    out=kct_sh.ap()[0][0, 0:64, bass.ds(tb_sv + 0, 128)],
                    in_=junk[0:64, 0:128]))
                print("POST-PREP DS OK")
            xt_s = [xt, ph0.enter_context(nc.sbuf_tensor("xt_b", [128, E], F32))]
            kcb_s = [kcb0, ph0.enter_context(nc.sbuf_tensor("kcb0_b", [128, 8, 128], BF16))]
            sqb_s = [sqb0, ph0.enter_context(nc.sbuf_tensor("sqb0_b", [128, 8, 128], BF16))]
            kvr_s = [kvarT0, ph0.enter_context(nc.sbuf_tensor("kvarT0_b", [2, 8, 128], BF16))]
            vbf_s = [vbf0, ph0.enter_context(nc.sbuf_tensor("vbf0_b", [128, E], BF16))]
            kv_tiles = [(b_o, 128 * sck) for b_o in range(B if _kvpart != "1" else 0)
                        for sck in range(8)]
            x_ready = [None, None]
            cons_v = [0, 0]
            cons_a = [0, 0]
            stg_cnt = [0, 0]
            sts_cnt = [0, 0]

            def ld_x_kv(i):
                slot = i % 2
                b_o, s0k = kv_tiles[i]
                nc.sync.wait_ge(V.sem, cons_v[slot])
                nc.sync.wait_ge(A.sem, cons_a[slot])
                ch_in.pre(nc.sync)
                ch_in.dma(nc.sync.dma_start(
                    out=xt_s[slot][:, :],
                    in_=x2_in.ap()[b_o, s0k:s0k + 128, :]))
                x_ready[slot] = ch_in.n

            if kv_tiles:
                ld_x_kv(0)
            for it, (b_o, s0k) in enumerate(kv_tiles):
                slot = it % 2
                if it + 1 < len(kv_tiles):
                    ld_x_kv(it + 1)
                nc.vector.wait_ge(ch_in.sem, x_ready[slot])
                nc.scalar.wait_ge(ch_in.sem, x_ready[slot])
                nc.vector.wait_ge(ch_stg.sem, stg_cnt[slot])
                nc.scalar.wait_ge(ch_stg.sem, stg_cnt[slot])
                nc.vector.wait_ge(ch_sts.sem, sts_cnt[slot])
                nc.scalar.wait_ge(ch_sts.sem, sts_cnt[slot])
                ln_stats(nc, V, A, xt_s[slot][:, :], mu, rr, work, junk)
                V.tensor_scalar(out=xn32k[:, :], in0=xt_s[slot][:, :],
                                scalar1=mu[:, 0:1], scalar2=rr[:, 0:1],
                                op0=ALU.subtract, op1=ALU.mult)
                transpose_f32(xnTk, xn32k, 8, ps_C)
                for dc in range(8):
                    for ec in range(8):
                        ins = nc.tensor.matmul(
                            ps_A[:, 128 * dc:128 * (dc + 1)],
                            wkc0[:, ec, 128 * dc:128 * (dc + 1)],
                            xnTk[:, ec, :],
                            start=(ec == 0), stop=(ec == 7))
                    ch_pe.inc(ins)
                pe_kc = ch_pe.n
                for nh in range(2):
                    for ec in range(8):
                        ins = nc.tensor.matmul(
                            ps_B[:, 512 * nh:512 * (nh + 1)],
                            xnTk[:, ec, :],
                            wv0[:, ec, 512 * nh:512 * (nh + 1)],
                            start=(ec == 0), stop=(ec == 7))
                    ch_pe.inc(ins)
                pe_v = ch_pe.n
                nc.vector.wait_ge(ch_pe.sem, pe_kc)
                nc.scalar.wait_ge(ch_pe.sem, pe_kc)
                V.tensor_copy(out=kcb_s[slot][:, 0:4, :],
                              in_=ps_A[:, 0:512].rearrange(
                                  "p (c f) -> p c f", f=128))
                nv0 = V.n
                V.tensor_copy(out=kcb_s[slot][:, 4:8, :],
                              in_=ps_A[:, 512:1024].rearrange(
                                  "p (c f) -> p c f", f=128))
                nv1 = V.n
                nc.scalar.wait_ge(V.sem, nv0)
                A.activation(out=sqb_s[slot][:, 0:4, :],
                             in_=ps_A[:, 0:512].rearrange(
                                 "p (c f) -> p c f", f=128),
                             func=AF.Square)
                nc.scalar.wait_ge(V.sem, nv1)
                A.activation(out=sqb_s[slot][:, 4:8, :],
                             in_=ps_A[:, 512:1024].rearrange(
                                 "p (c f) -> p c f", f=128),
                             func=AF.Square)
                a_sq = A.n
                nc.vector.wait_ge(ch_pe.sem, pe_v)
                V.tensor_copy(out=vbf_s[slot][:, :], in_=ps_B[:, :])
                nc.tensor.wait_ge(A.sem, a_sq)
                for c in range(8):
                    ins = nc.tensor.matmul(
                        ps_C[0:2, 128 * c:128 * (c + 1)], onesb[:, :],
                        sqb_s[slot][:, c, :], start=True, stop=True)
                ch_pe.inc(ins)
                nc.vector.wait_ge(ch_pe.sem, ch_pe.n)
                V.tensor_scalar(out=kvr_s[slot][:, :, :],
                                in0=ps_C[0:2, :].rearrange(
                                    "p (c f) -> p c f", f=128),
                                scalar1=1.0 / 63.0, scalar2=None,
                                op0=ALU.mult)
                cons_v[slot] = V.n
                cons_a[slot] = A.n
                V.wait(nc.gpsimd)
                V.wait(nc.sync)
                kvv = kct_pr.ap()[b_o]
                kvh = kvv.rearrange("(c j) s t -> j c s t", j=2)
                nc.sync.dma_start(
                    out=kvh[0][:, 0:64, s0k:s0k + 128].rearrange(
                        "c s t -> s c t"),
                    in_=kcb_s[slot][0:64, :, :]).then_inc(nullsem2, 16)
                ch_sts.dma(nc.sync.dma_start(
                    out=kvh[1][:, 0:64, s0k:s0k + 128].rearrange(
                        "c s t -> s c t"),
                    in_=kcb_s[slot][64:128, :, :]))
                sts_cnt[slot] = ch_sts.n
                nc.gpsimd.dma_start(
                    out=kvv[:, 64, s0k:s0k + 128].rearrange(
                        "(c p) t -> p c t", p=2),
                    in_=kvr_s[slot][:, :, :]).then_inc(nullsem, 16)
                nc.gpsimd.dma_start(
                    out=v_pr.ap()[b_o, 0:8, s0k:s0k + 128, :]
                    .rearrange("h t d -> t h d"),
                    in_=vbf_s[slot][:, :].rearrange("t (h d) -> t h d", d=D)
                    [:, 0:8, :]).then_inc(nullsem, 16)
                ch_stg.dma(nc.gpsimd.dma_start(
                    out=v_pr.ap()[b_o, 8:16, s0k:s0k + 128, :]
                    .rearrange("h t d -> t h d"),
                    in_=vbf_s[slot][:, :].rearrange("t (h d) -> t h d", d=D)
                    [:, 8:16, :]))
                stg_cnt[slot] = ch_stg.n
            ch_stg.pre(nc.gpsimd)
            ch_sts.wait(nc.gpsimd)
            V.wait(nc.gpsimd)
            ch_cp = Ch(nc, "ch_cp")
            for arm in (nc.switch_core_id([nc.gpsimd], n=128)
                        if _kvpart != "2" else []):
                lg = {4: 0, 5: 1, 6: 2, 7: 3, 2: 4, 3: 5, 0: 6, 1: 7}[int(arm) % 8]
                t0 = (lg % 2) * (T // 2)
                for b_o in range(B):
                    nc.gpsimd.dma_start(
                        out=kct_sh.ap()[b_o, :, :, t0:t0 + T // 2],
                        in_=kct_pr.ap()[b_o]).then_inc(ch_cp.sem, 16)
                    nc.gpsimd.dma_start(
                        out=v_sh.ap()[b_o, :, t0:t0 + T // 2, :],
                        in_=v_pr.ap()[b_o]).then_inc(ch_cp.sem, 16)
                nc.gpsimd.wait_ge(ch_cp.sem, 64 * 16)
                nc.gpsimd.dma_start(
                    out=flag_sh.ap()[lg % 2, :][None, :],
                    in_=fone[0:1, :]).then_inc(ch_cp.sem, 16)
            ch_cp.n += 65 * 16
            fence()
            ph0.close()

            ch_ing.wait(nc.tensor)
            ch_in.wait(nc.tensor)
            if stage >= 1:
                xtp_s = [ph1.enter_context(nc.sbuf_tensor(f"xtp{i}", [128, E], F32)) for i in range(2)]
                qtb_s = [qtb, ph1.enter_context(nc.sbuf_tensor("qtb_b", [128, 8, 128], BF16))]
                qvr_s = [qvarT, ph1.enter_context(nc.sbuf_tensor("qvarT_b", [2, 8, 128], BF16))]
                qkv_s = [qkvb, ph1.enter_context(nc.sbuf_tensor("qkvb_b", [128, 3 * E], BF16))]
                xn_s = [xn32, ph1.enter_context(nc.sbuf_tensor("xn32_b", [128, E], F32))]
                p1_tiles = [(b_o, 128 * sc) for b_o in range(B)
                            for sc in range(SLOC // 128)]
                x_ready1 = [None, None]
                cons_v1 = [0, 0]
                cons_a1 = [0, 0]
                st_cnt1 = [0, 0]

                def ld_x_p1(i):
                    slot = i % 2
                    b_o, s0 = p1_tiles[i]
                    nc.sync.wait_ge(V.sem, cons_v1[slot])
                    nc.sync.wait_ge(A.sem, cons_a1[slot])
                    ch_in.pre(nc.sync)
                    ch_in.dma(nc.sync.dma_start(
                        out=xtp_s[slot][:, :],
                        in_=x_in.ap()[b_o, s0:s0 + 128, :]))
                    x_ready1[slot] = ch_in.n

                ld_x_p1(0)
                for it, (b_o, s0) in enumerate(p1_tiles):
                    slot = it % 2
                    if it + 1 < len(p1_tiles):
                        ld_x_p1(it + 1)
                    nc.vector.wait_ge(ch_in.sem, x_ready1[slot])
                    nc.scalar.wait_ge(ch_in.sem, x_ready1[slot])
                    nc.vector.wait_ge(ch_st.sem, st_cnt1[slot])
                    nc.scalar.wait_ge(ch_st.sem, st_cnt1[slot])
                    xt1 = xtp_s[slot]
                    ln_stats(nc, V, A, xt1[:, :], mu, rr, work, junk)
                    V.reciprocal(out=work[:, 7:8], in_=work[:, 5:6])
                    V.tensor_tensor(out=work[:, 8:9], in0=work[:, 4:5],
                                    in1=work[:, 7:8], op=ALU.mult)
                    V.tensor_scalar(out=work[:, 9:10], in0=work[:, 8:9],
                                    scalar1=EPS, scalar2=None, op0=ALU.add)
                    V.wait(nc.scalar)
                    A.activation(out=work[:, 10:11], in_=work[:, 9:10],
                                 func=AF.Sqrt)
                    A.wait(nc.vector)
                    V.reciprocal(out=r2[:, 0:1], in_=work[:, 10:11])
                    V.tensor_scalar(out=xn_s[slot][:, :], in0=xt1[:, :],
                                    scalar1=mu[:, 0:1], scalar2=rr[:, 0:1],
                                    op0=ALU.subtract, op1=ALU.mult)
                    transpose_f32(xnT, xn_s[slot], 8, ps_C)
                    for dc in range(8):
                        for ec in range(8):
                            ins = nc.tensor.matmul(
                                ps_A[:, 128 * dc:128 * (dc + 1)],
                                wq[:, ec, 128 * dc:128 * (dc + 1)],
                                xnT[:, ec, :],
                                start=(ec == 0), stop=(ec == 7))
                        ch_pe.inc(ins)
                    pe_q = ch_pe.n
                    for nh in range(2):
                        for ec in range(8):
                            ins = nc.tensor.matmul(
                                ps_B[:, 512 * nh:512 * (nh + 1)],
                                xnT[:, ec, :],
                                ipT[:, ec, 512 * nh:512 * (nh + 1)],
                                start=(ec == 0), stop=(ec == 7))
                        ch_pe.inc(ins)
                    pe_k1 = ch_pe.n
                    nc.vector.wait_ge(ch_pe.sem, pe_q)
                    nc.scalar.wait_ge(ch_pe.sem, pe_q)
                    V.tensor_copy(out=qtb_s[slot][:, 0:4, :],
                                  in_=ps_A[:, 0:512].rearrange(
                                      "p (c f) -> p c f", f=128))
                    ndve0 = V.n
                    V.tensor_copy(out=qtb_s[slot][:, 4:8, :],
                                  in_=ps_A[:, 512:1024].rearrange(
                                      "p (c f) -> p c f", f=128))
                    ndve1 = V.n
                    nc.scalar.wait_ge(V.sem, ndve0)
                    A.activation(out=sqb[:, 0:4, :],
                                 in_=ps_A[:, 0:512].rearrange(
                                     "p (c f) -> p c f", f=128),
                                 func=AF.Square)
                    nc.scalar.wait_ge(V.sem, ndve1)
                    A.activation(out=sqb[:, 4:8, :],
                                 in_=ps_A[:, 512:1024].rearrange(
                                     "p (c f) -> p c f", f=128),
                                 func=AF.Square)
                    a_sq = A.n
                    nc.vector.wait_ge(ch_pe.sem, pe_k1)
                    V.scalar_tensor_tensor(
                        out=qkv_s[slot][:, 0:1024], in0=ps_B[:, :],
                        scalar=r2[:, 0:1], in1=brep[:, 0:1024],
                        op0=ALU.mult, op1=ALU.add)
                    v_k1 = V.n
                    for nh in range(2):
                        for ec in range(8):
                            ins = nc.tensor.matmul(
                                ps_C[:, 512 * nh:512 * (nh + 1)],
                                xnT[:, ec, :],
                                ipT[:, ec, 1024 + 512 * nh:1536 + 512 * nh],
                                start=(ec == 0), stop=(ec == 7))
                        ch_pe.inc(ins)
                    pe_k2 = ch_pe.n
                    nc.tensor.wait_ge(A.sem, a_sq)
                    for rnd in range(2):
                        for cc in range(4):
                            c = 4 * rnd + cc
                            ins = nc.tensor.matmul(
                                ps_D[0:2, 128 * cc:128 * (cc + 1)],
                                onesb[:, :], qtb_s[slot][:, c, :],
                                start=True, stop=True)
                            ins2 = nc.tensor.matmul(
                                ps_E[0:2, 128 * cc:128 * (cc + 1)],
                                onesb[:, :], sqb[:, c, :],
                                start=True, stop=True)
                        ch_pe.inc(ins2)
                        nc.vector.wait_ge(ch_pe.sem, ch_pe.n)
                        V.tensor_copy(
                            out=stq[:, 4 * rnd:4 * rnd + 4, :],
                            in_=ps_D[0:2, 0:512].rearrange(
                                "p (c f) -> p c f", f=128))
                        V.tensor_copy(
                            out=sts[:, 4 * rnd:4 * rnd + 4, :],
                            in_=ps_E[0:2, 0:512].rearrange(
                                "p (c f) -> p c f", f=128))
                        vst = V.n
                        if rnd == 0:
                            nc.tensor.wait_ge(V.sem, vst)
                    nc.vector.wait_ge(ch_pe.sem, pe_k2)
                    V.scalar_tensor_tensor(
                        out=qkv_s[slot][:, 1024:2048], in0=ps_C[:, :],
                        scalar=r2[:, 0:1], in1=brep[:, 1024:2048],
                        op0=ALU.mult, op1=ALU.add)
                    for nh in range(2):
                        for ec in range(8):
                            ins = nc.tensor.matmul(
                                ps_B[:, 512 * nh:512 * (nh + 1)],
                                xnT[:, ec, :],
                                ipT[:, ec, 2048 + 512 * nh:2560 + 512 * nh],
                                start=(ec == 0), stop=(ec == 7))
                        ch_pe.inc(ins)
                    pe_k3 = ch_pe.n
                    V.tensor_tensor(out=st3[:, :, :], in0=stq[:, :, :],
                                    in1=stq[:, :, :], op=ALU.mult)
                    V.tensor_scalar(out=st3[:, :, :], in0=st3[:, :, :],
                                    scalar1=1.0 / (64.0 * 63.0),
                                    scalar2=None, op0=ALU.mult)
                    V.tensor_scalar(out=stq[:, :, :], in0=sts[:, :, :],
                                    scalar1=1.0 / 63.0, scalar2=None,
                                    op0=ALU.mult)
                    V.tensor_tensor(out=qvr_s[slot][:, :, :],
                                    in0=stq[:, :, :],
                                    in1=st3[:, :, :], op=ALU.subtract)
                    nc.vector.wait_ge(ch_pe.sem, pe_k3)
                    V.scalar_tensor_tensor(
                        out=qkv_s[slot][:, 2048:3072], in0=ps_B[:, :],
                        scalar=r2[:, 0:1], in1=brep[:, 2048:3072],
                        op0=ALU.mult, op1=ALU.add)
                    cons_v1[slot] = V.n
                    cons_a1[slot] = A.n
                    V.wait(nc.sync)
                    qv = qt_dr.ap()[b_o]
                    for c in range(8):
                        for j in range(2):
                            nc.sync.dma_start(
                                out=qv[2 * c + j, 0:64, s0:s0 + 128],
                                in_=qtb_s[slot][64 * j:64 * j + 64, c, :]
                            ).then_inc(nullsem2, 16)
                    nc.sync.dma_start(
                        out=qv[:, 64, s0:s0 + 128].rearrange(
                            "(c p) t -> p c t", p=2),
                        in_=qvr_s[slot][:, :, :]).then_inc(nullsem2, 16)
                    nc.sync.dma_start(
                        out=qkv_dr.ap()[s0:s0 + 128, b_o, :, :, :]
                        .rearrange("t q h d -> t (q h d)"),
                        in_=qkv_s[slot][:, :]).then_inc(nullsem2, 16)
                    ch_st.dma(nc.sync.dma_start(
                        out=xn_dr.ap()[s0:s0 + 128, b_o, :],
                        in_=xn_s[slot][:, :]))
                    st_cnt1[slot] = ch_st.n
            fence()
            ph1.close()
            if stage >= 2:
                ph2 = ExitStack()
                opjT = ph2.enter_context(nc.sbuf_tensor("opjT", [128, 8, E], BF16))
                opbr = ph2.enter_context(nc.sbuf_tensor("opbr", [128, E], F32))
                rwT32 = ph2.enter_context(nc.sbuf_tensor("rwT32", [128, 8, 8], F32))
                rwThi = ph2.enter_context(nc.sbuf_tensor("rwThi", [128, 8, 8], BF16))
                rwThi32 = ph2.enter_context(nc.sbuf_tensor("rwThi32", [128, 8, 8], F32))
                rwTlo = ph2.enter_context(nc.sbuf_tensor("rwTlo", [128, 8, 8], BF16))
                rw8 = ph2.enter_context(nc.sbuf_tensor("rw8", [8, E], F32))
                qib = ph2.enter_context(nc.sbuf_tensor("qib", [128, E], BF16))
                kib = ph2.enter_context(nc.sbuf_tensor("kib", [128, E], BF16))
                vib = ph2.enter_context(nc.sbuf_tensor("vib", [128, E], BF16))
                qTt = ph2.enter_context(nc.sbuf_tensor("qTt", [128, E], BF16))
                kTt = ph2.enter_context(nc.sbuf_tensor("kTt", [128, E], BF16))
                xng = ph2.enter_context(nc.sbuf_tensor("xng", [128, E], F32))
                exb = ph2.enter_context(nc.sbuf_tensor("exb", [128, H, 32], BF16))
                pmb = ph2.enter_context(nc.sbuf_tensor("pmb", [128, H, 32], BF16))
                ptb = ph2.enter_context(nc.sbuf_tensor("ptb", [128, H, 32], BF16))
                sums = ph2.enter_context(nc.sbuf_tensor("sums", [128, H], F32))
                recs = ph2.enter_context(nc.sbuf_tensor("recs", [128, H], F32))
                at32 = ph2.enter_context(nc.sbuf_tensor("at32", [128, E], F32))
                atT = ph2.enter_context(nc.sbuf_tensor("atT", [128, 8, 128], BF16))
                x1sb = ph2.enter_context(nc.sbuf_tensor("x1sb", [128, E], F32))
                z32 = ph2.enter_context(nc.sbuf_tensor("z32", [128, E], F32))
                hnT32 = ph2.enter_context(nc.sbuf_tensor("hnT32", [128, 8, 128], F32))
                hnbf = ph2.enter_context(nc.sbuf_tensor("hnbf", [128, 8, 128], BF16))
                hnlo = ph2.enter_context(nc.sbuf_tensor("hnlo", [128, 8, 128], BF16))
                rpb = ph2.enter_context(nc.sbuf_tensor("rpb", [128, 8], F32))
                rp2b = ph2.enter_context(nc.sbuf_tensor("rp2b", [128, 8], F32))

                for i in range(8):
                    ch_pe.wait(nc.sync)
                    ld_sync(tmp32[:, :], opw_in.ap()[128 * i:128 * (i + 1), :])
                    ch_in.wait(nc.tensor)
                    V.wait(nc.tensor)
                    for j in range(8):
                        ins = nc.tensor.transpose(
                            ps_C[:, 512 * (j % 2):512 * (j % 2) + 128],
                            tmp32[:, 128 * j:128 * (j + 1)], identf[:, :])
                        ch_pe.inc(ins)
                        ch_pe.wait(nc.vector)
                        V.tensor_copy(out=opjT[:, j, 128 * i:128 * (i + 1)],
                                      in_=ps_C[:, 512 * (j % 2):512 * (j % 2) + 128])
                        V.wait(nc.tensor)
                ld_sync(ipbrow[0:1, 0:E], opb_in.ap()[None, :])
                ch_in.wait(nc.vector)
                V.tensor_copy(out=opbr[0:1, :], in_=ipbrow[0:1, 0:E])
                V.wait(nc.sync)
                rep = 1
                while rep < 128:
                    ch_st.pre(nc.sync)
                    ch_st.dma(nc.sync.dma_start(
                        out=opbr[rep:2 * rep, :], in_=opbr[0:rep, :]))
                    rep *= 2
                ch_st.wait(nc.vector)
                ld_sync(rw8[:, :], rw_in.ap())
                ch_in.wait(nc.tensor)
                for c in range(8):
                    ch_pe.wait(nc.vector)
                    ins = nc.tensor.transpose(
                        ps_E[:, 0:8], rw8[:, 128 * c:128 * (c + 1)],
                        identf[0:8, 0:8])
                    ch_pe.inc(ins)
                    ch_pe.wait(nc.vector)
                    V.tensor_copy(out=rwT32[:, c, :], in_=ps_E[:, 0:8])
                    V.wait(nc.tensor)
                V.tensor_copy(out=rwThi[:, :, :], in_=rwT32[:, :, :])
                V.tensor_copy(out=rwThi32[:, :, :], in_=rwThi[:, :, :])
                V.tensor_tensor(out=rwTlo[:, :, :], in0=rwT32[:, :, :],
                                in1=rwThi32[:, :, :], op=ALU.subtract)

                pm_flat = pmb[:, :, :].rearrange("p n t -> p (n t)")
                pt_flat = ptb[:, :, :].rearrange("p n t -> p (n t)")
                qib_s = [qib, ph2.enter_context(nc.sbuf_tensor("qib_b", [128, E], BF16))]
                kib_s = [kib, ph2.enter_context(nc.sbuf_tensor("kib_b", [128, E], BF16))]
                vib_s = [vib, ph2.enter_context(nc.sbuf_tensor("vib_b", [128, E], BF16))]
                xng_s = [xng, ph2.enter_context(nc.sbuf_tensor("xng_b", [128, E], F32))]
                g_ready = [None, None]
                g_cons_v = [0, 0]
                g_cons_pe = [0, 0]

                def ld_grp(g2):
                    slot = g2 % 2
                    nc.sync.wait_ge(V.sem, g_cons_v[slot])
                    nc.sync.wait_ge(ch_pe.sem, g_cons_pe[slot])
                    for buf, qidx in ((qib_s[slot], 0), (kib_s[slot], 1),
                                      (vib_s[slot], 2)):
                        ch_in.pre(nc.sync)
                        ch_in.dma(nc.sync.dma_start(
                            out=buf[:, :],
                            in_=qkv_dr.ap()[4 * g2:4 * g2 + 4, :, qidx, :, :]
                            .rearrange("a b h d -> (a b) (h d)")))
                    ch_in.pre(nc.sync)
                    ch_in.dma(nc.sync.dma_start(
                        out=xng_s[slot][:, :],
                        in_=xn_dr.ap()[4 * g2:4 * g2 + 4, :, :]
                        .rearrange("a b e -> (a b) e")))
                    g_ready[slot] = ch_in.n

                ld_grp(0)
                for g in range(64):
                    gslot = g % 2
                    if g + 1 < 64:
                        ld_grp(g + 1)
                    qib, kib, vib, xng = (qib_s[gslot], kib_s[gslot],
                                          vib_s[gslot], xng_s[gslot])
                    ch_st.wait(nc.vector)
                    ch_st.wait(nc.scalar)
                    nc.vector.wait_ge(ch_in.sem, g_ready[gslot])
                    V.transpose(out=qTt[:, :], in_=qib[:, :])
                    V.transpose(out=kTt[:, :], in_=kib[:, :])
                    V.wait(nc.tensor)
                    A.wait(nc.tensor)
                    for b in range(4):
                        for n in range(H):
                            for hf in range(2):
                                ins = nc.tensor.matmul(
                                    ps_D[32 * b:32 * b + 32,
                                         32 * n:32 * n + 32],
                                    qTt[32 * b:32 * b + 32,
                                        64 * n + 32 * hf:64 * n + 32 * hf + 32],
                                    kTt[32 * b:32 * b + 32,
                                        64 * n + 32 * hf:64 * n + 32 * hf + 32],
                                    start=(hf == 0), stop=(hf == 1),
                                    tile_position=(32 * b, 32 * b),
                                    skip_group_check=True)
                        ch_pe.inc(ins)
                    ch_pe.wait(nc.scalar)
                    A.activation(out=exb[:, :, :],
                                 in_=ps_D[:, :].rearrange(
                                     "p (n t) -> p n t", t=32),
                                 func=AF.Exp, scale=0.125)
                    A.wait(nc.vector)
                    V.tensor_tensor(out=pmb[:, :, :], in0=exb[:, :, :],
                                    in1=mask[:, :, :], op=ALU.mult)
                    V.tensor_reduce(out=sums[:, :], in_=pmb[:, :, :],
                                    axis=AX.X, op=ALU.add)
                    V.reciprocal(out=recs[:, :], in_=sums[:, :])
                    V.transpose(out=pt_flat, in_=pm_flat)
                    V.wait(nc.tensor)
                    for b in range(4):
                        for n in range(H):
                            ins = nc.tensor.matmul(
                                ps_A[32 * b:32 * b + 32, 64 * n:64 * n + 64],
                                ptb[32 * b:32 * b + 32, n, :],
                                vib[32 * b:32 * b + 32, 64 * n:64 * n + 64],
                                start=True, stop=True,
                                tile_position=(32 * b, 32 * b),
                                skip_group_check=True)
                        ch_pe.inc(ins)
                    ch_pe.wait(nc.vector)
                    for n in range(H):
                        V.tensor_scalar(out=at32[:, 64 * n:64 * n + 64],
                                        in0=ps_A[:, 64 * n:64 * n + 64],
                                        scalar1=recs[:, n:n + 1],
                                        scalar2=None, op0=ALU.mult)
                    transpose_f32(atT, at32, 8, ps_B)
                    for half in range(2):
                        for c in range(8):
                            ins = nc.tensor.matmul(
                                ps_C[:, 512 * half:512 * half + 512],
                                atT[:, c, :],
                                opjT[:, c, 512 * half:512 * half + 512],
                                start=(c == 0), stop=(c == 7))
                        ch_pe.inc(ins)
                    ch_pe.wait(nc.vector)
                    V.tensor_tensor(out=x1sb[:, :], in0=ps_C[:, :],
                                    in1=opbr[:, :], op=ALU.add)
                    V.tensor_tensor(out=x1sb[:, :], in0=x1sb[:, :],
                                    in1=xng[:, :], op=ALU.add)
                    V.wait(nc.sync)
                    st_sync(x1_dr.ap()[4 * g:4 * g + 4, :, :]
                            .rearrange("a b e -> (a b) e"), x1sb[:, :])
                    ln_stats(nc, V, A, x1sb[:, :], mu, rr, work, junk)
                    V.tensor_scalar(out=z32[:, :], in0=x1sb[:, :],
                                    scalar1=mu[:, 0:1], scalar2=rr[:, 0:1],
                                    op0=ALU.subtract, op1=ALU.mult)
                    V.wait(nc.tensor)
                    A.wait(nc.tensor)
                    for c in range(8):
                        ins = nc.tensor.transpose(
                            ps_B[:, 128 * c:128 * (c + 1)],
                            z32[:, 128 * c:128 * (c + 1)], identf[:, :])
                    ch_pe.inc(ins)
                    ch_pe.wait(nc.vector)
                    for c in range(8):
                        V.tensor_scalar(out=hnT32[:, c, :],
                                        in0=ps_B[:, 128 * c:128 * (c + 1)],
                                        scalar1=ln2w[:, c:c + 1],
                                        scalar2=ln2b[:, c:c + 1],
                                        op0=ALU.mult, op1=ALU.add)
                    V.wait(nc.tensor)
                    V.tensor_copy(out=hnbf[:, :, :], in_=hnT32[:, :, :])
                    V.tensor_tensor(out=hnlo[:, :, :], in0=hnT32[:, :, :],
                                    in1=hnbf[:, :, :], op=ALU.subtract)
                    V.wait(nc.sync)
                    st_sync(hn2_dr.ap()[g // 4, :, :,
                                        128 * (g % 4):128 * (g % 4) + 128]
                            .rearrange("c p f -> p c f"),
                            hnbf[:, :, :])
                    V.wait(nc.tensor)
                    first = True
                    for lh, rh in ((hnbf, rwThi), (hnbf, rwTlo),
                                   (hnlo, rwThi)):
                        for c in range(8):
                            ins = nc.tensor.matmul(
                                ps_E[:, 0:8], lh[:, c, :], rh[:, c, :],
                                start=first, stop=(lh is hnlo and c == 7))
                            first = False
                    ch_pe.inc(ins)
                    ch_pe.wait(nc.vector)
                    V.tensor_reduce(out=work[:, 11:12], in_=ps_E[:, 0:8],
                                    axis=AX.X, op=ALU.max, negate=True)
                    V.wait(nc.scalar)
                    A.activation(out=rpb[:, :], in_=ps_E[:, 0:8], func=AF.Exp,
                                 bias=work[:, 11:12],
                                 accum_out=work[:, 12:13])
                    A.wait(nc.vector)
                    V.reciprocal(out=work[:, 13:14], in_=work[:, 12:13])
                    V.tensor_scalar(out=rpb[:, :], in0=rpb[:, :],
                                    scalar1=work[:, 13:14], scalar2=None,
                                    op0=ALU.mult)
                    V.tensor_reduce(out=work[:, 14:15], in_=rpb[:, :],
                                    axis=AX.X, op=ALU.max)
                    V.scalar_tensor_tensor(out=rp2b[:, :], in0=rpb[:, :],
                                           scalar=work[:, 14:15],
                                           in1=rpb[:, :],
                                           op0=ALU.is_lt, op1=ALU.mult)
                    V.tensor_reduce(out=work[:, 15:16], in_=rp2b[:, :],
                                    axis=AX.X, op=ALU.max)
                    V.scalar_tensor_tensor(out=gates[:, 8 * g:8 * g + 8],
                                           in0=rpb[:, :],
                                           scalar=work[:, 15:16],
                                           in1=rpb[:, :],
                                           op0=ALU.is_ge, op1=ALU.mult)
                    g_cons_v[gslot] = V.n
                    g_cons_pe[gslot] = ch_pe.n

            if stage >= 2:
                fence()
                ph2.close()
            if stage >= 3:
                ph3 = ExitStack()
                w1b = [ph3.enter_context(nc.sbuf_tensor(f"w1b{i}", [128, 8, FF], BF16))
                       for i in range(2)]
                w2b = [ph3.enter_context(nc.sbuf_tensor(f"w2b{i}", [128, 16, E], BF16))
                       for i in range(2)]
                hb = ph3.enter_context(nc.sbuf_tensor("hb", [128, 8, 512], BF16))
                hmidT = ph3.enter_context(nc.sbuf_tensor("hmidT", [128, 16, 512], BF16))
                ysb = ph3.enter_context(nc.sbuf_tensor("ysb", [128, E], F32))
                ysb2 = ph3.enter_context(nc.sbuf_tensor("ysb2", [128, E], F32))

                def ld_expert(par, e):
                    ch_ing.pre(nc.gpsimd)
                    ch_ing.dma(nc.gpsimd.dma_start(
                        out=w1b[par][:, :, :],
                        in_=w1_in.ap()[e].rearrange("(c p) f -> p c f",
                                                    p=128)))
                    ch_ing.pre(nc.gpsimd)
                    ch_ing.dma(nc.gpsimd.dma_start(
                        out=w2b[par][:, :, :],
                        in_=w2_in.ap()[e].rearrange("(c p) f -> p c f",
                                                    p=128)))

                ld_expert(0, 0)
                hb2 = [hb, ph3.enter_context(nc.sbuf_tensor("hb_b", [128, 8, 512], BF16))]
                ysbs = [ysb, ph3.enter_context(nc.sbuf_tensor("ysb_b", [128, E], F32))]
                psDE = (ps_D, ps_E)
                psAB = (ps_A, ps_B)
                hb_users = [0, 0]
                hb_ready = [None, None]
                y_ready = {}
                y_store_cnt = {}
                yld_users = [0, 0]

                def ld_hb(sg2):
                    slot = sg2 % 2
                    nc.sync.wait_ge(ch_pe.sem, hb_users[slot])
                    ch_in.pre(nc.sync)
                    ch_in.dma(nc.sync.dma_start(
                        out=hb2[slot][:, :, :],
                        in_=hn2_dr.ap()[sg2].rearrange("c p f -> p c f")))
                    hb_ready[slot] = ch_in.n

                def ld_y(e2, sg2, tc2):
                    slot = tc2 % 2
                    nc.sync.wait_ge(V.sem, yld_users[slot])
                    prior = y_store_cnt.get((sg2, tc2))
                    if prior is not None:
                        nc.sync.wait_ge(ch_st.sem, prior)
                    ch_in.pre(nc.sync)
                    ch_in.dma(nc.sync.dma_start(
                        out=ysbs[slot][:, :],
                        in_=y_dr.ap()[512 * sg2 + 128 * tc2:
                                      512 * sg2 + 128 * tc2 + 128, :]))
                    y_ready[(e2, sg2, tc2)] = ch_in.n

                ld_hb(0)
                for e in range(NEXP):
                    par = e % 2
                    wready = ch_ing.n
                    if e + 1 < NEXP:
                        nc.gpsimd.wait_ge(ch_pe.sem, ch_pe.n)
                        ld_expert(1 - par, e + 1)
                    nc.tensor.wait_ge(ch_ing.sem, wready)
                    for sg in range(16):
                        hslot = sg % 2
                        if (sg + 1 < 16) or (e + 1 < NEXP):
                            ld_hb((sg + 1) % 16)
                        nc.tensor.wait_ge(ch_in.sem, hb_ready[hslot])
                        if e > 0:
                            ld_y(e, sg, 0)
                        gelu_done = []
                        for fc in range(16):
                            pd = psDE[fc % 2]
                            if fc >= 2:
                                nc.tensor.wait_ge(A.sem, gelu_done[fc - 2])
                            for kc in range(8):
                                ins = nc.tensor.matmul(
                                    pd[:, :],
                                    w1b[par][:, kc, 128 * fc:128 * (fc + 1)],
                                    hb2[hslot][:, kc, :],
                                    start=(kc == 0), stop=(kc == 7))
                            ch_pe.inc(ins)
                            ch_pe.wait(nc.scalar)
                            A.activation(out=hmidT[:, fc, :], in_=pd[:, :],
                                         func=AF.Gelu)
                            gelu_done.append(A.n)
                        hb_users[hslot] = ch_pe.n
                        nc.tensor.wait_ge(A.sem, gelu_done[15])
                        stt_done = []
                        for tc in range(4):
                            pa = psAB[tc % 2]
                            gcol = 8 * (4 * sg + tc) + e
                            row0 = 512 * sg + 128 * tc
                            if e > 0 and tc + 1 < 4:
                                ld_y(e, sg, tc + 1)
                            if tc >= 2:
                                nc.tensor.wait_ge(V.sem, stt_done[tc - 2])
                            for fc in range(16):
                                for half in range(2):
                                    ins = nc.tensor.matmul(
                                        pa[:, 512 * half:512 * half + 512],
                                        hmidT[:, fc, 128 * tc:128 * (tc + 1)],
                                        w2b[par][:, fc,
                                                 512 * half:512 * half + 512],
                                        start=(fc == 0), stop=(fc == 15))
                            ch_pe.inc(ins)
                            ch_pe.wait(nc.vector)
                            if e > 0:
                                nc.vector.wait_ge(ch_in.sem,
                                                  y_ready[(e, sg, tc)])
                                V.scalar_tensor_tensor(
                                    out=ysb2[:, :], in0=pa[:, :],
                                    scalar=gates[:, gcol:gcol + 1],
                                    in1=ysbs[tc % 2][:, :],
                                    op0=ALU.mult, op1=ALU.add)
                                yld_users[tc % 2] = V.n
                            else:
                                V.tensor_scalar(
                                    out=ysb2[:, :], in0=pa[:, :],
                                    scalar1=gates[:, gcol:gcol + 1],
                                    scalar2=None, op0=ALU.mult)
                            stt_done.append(V.n)
                            V.wait(nc.sync)
                            st_sync(y_dr.ap()[row0:row0 + 128, :], ysb2[:, :])
                            y_store_cnt[(sg, tc)] = ch_st.n
                fence()
                ph3.close()

            if stage >= 4:
                ph4 = ExitStack()
                wo_nat = ph4.enter_context(nc.sbuf_tensor("wonat", [128, 8, E], BF16))
                kbufs = [ph4.enter_context(nc.sbuf_tensor(f"kbuf{i}", [65, 8, T], BF16)) for i in range(2)]
                vbufs = [ph4.enter_context(nc.sbuf_tensor(f"vbuf{i}", [128, 16, 8, 65], BF16)) for i in range(2)]
                qbuf = ph4.enter_context(nc.sbuf_tensor("qbuf", [65, H, SLOC], BF16))
                pTs = [ph4.enter_context(nc.sbuf_tensor(f"pts{i}", [128, 16, SLOC], BF16)) for i in range(2)]
                a32 = ph4.enter_context(nc.sbuf_tensor("a32", [128, 2, E], F32))
                atT2 = ph4.enter_context(nc.sbuf_tensor("atT2", [128, 8, 128], BF16))
                recb = ph4.enter_context(nc.sbuf_tensor("recb", [128, 2], F32))
                xg = ph4.enter_context(nc.sbuf_tensor("xg", [128, E], F32))
                x1g = ph4.enter_context(nc.sbuf_tensor("x1g", [128, E], F32))
                yg = ph4.enter_context(nc.sbuf_tensor("yg", [128, E], F32))
                outsb = ph4.enter_context(nc.sbuf_tensor("outsb", [128, E], F32))

                ch_ing.pre(nc.gpsimd)
                ch_ing.dma(nc.gpsimd.dma_start(
                    out=wo_nat[:, :, :],
                    in_=wo_in.ap().rearrange("(c p) d -> p c d", p=128)))
                for i in range(2):
                    V.memset(vbufs[i][:, :, :, 64:65], 1.0)
                nc.tensor.wait_ge(ch_ing.sem, ch_ing.n)
                psem = nc.alloc_semaphore("pollsem")
                V.wait(nc.gpsimd)
                ch_stg.wait(nc.gpsimd)
                ch_cp.wait(nc.gpsimd)
                with nc.gpsimd.register("fv") as freg, \
                     nc.gpsimd.register("f2") as freg2, \
                     nc.gpsimd.register("pk") as kreg:
                    nc.gpsimd.reg_mov(freg, 0)
                    nc.gpsimd.reg_mov(freg2, 0)
                    nc.gpsimd.reg_mov(kreg, 0)

                    def _cond():
                        a = nc.gpsimd.snap(freg, min_val=0, max_val=10,
                                           donate=True)
                        b2 = nc.gpsimd.snap(freg2, min_val=0, max_val=10,
                                            donate=True)
                        return a + b2 < 2
                    with nc.gpsimd.While(_cond):
                        nc.gpsimd.reg_alu(
                            kreg, nc.gpsimd.snap(kreg, min_val=0,
                                                 max_val=10**9), 16, ALU.add)
                        nc.gpsimd.dma_start(
                            out=fbz[0:2, :], in_=flag_sh.ap()[:, :]
                        ).then_inc(psem, 16)
                        nc.gpsimd.wait_ge(
                            psem, nc.gpsimd.snap(kreg, min_val=0,
                                                 max_val=10**9))
                        nc.reg_load(freg, fbz[0:1, 0:1])
                        nc.reg_load(freg2, fbz[1:2, 0:1])
                items = [(b_o, hf) for b_o in range(B) for hf in range(2)]
                kv_ready = []

                def ld_kv(i):
                    b_o, hf = items[i]
                    slot = i % 2
                    nc.gpsimd.wait_ge(ch_pe.sem, ch_pe.n)
                    V.wait(nc.gpsimd)
                    ch_ing.pre(nc.gpsimd)
                    ch_ing.dma(nc.gpsimd.dma_start(
                        out=kbufs[slot][:, :, :],
                        in_=kct_sh.ap()[b_o, 8 * hf:8 * hf + 8, :, :]
                        .rearrange("h d t -> d h t")))
                    for hh8 in range(8):
                        ch_ing.pre(nc.gpsimd)
                        ch_ing.dma(nc.gpsimd.dma_start(
                            out=vbufs[slot][:, :, hh8, 0:64],
                            in_=v_sh.ap()[b_o, 8 * hf + hh8, :, :]
                            .rearrange("(c p) d -> p c d", p=128)))
                    kv_ready.append(ch_ing.n)

                ld_kv(0)
                exp_on_slot = [0, 0]
                psAB2 = (ps_A, ps_B)
                qbufs = [qbuf, ph4.enter_context(nc.sbuf_tensor("qbuf_b", [65, H, SLOC], BF16))]
                q_ready = [None, None]
                q_users = [0, 0]

                def ld_q(b_o2):
                    slot = b_o2 % 2
                    nc.sync.wait_ge(ch_pe.sem, q_users[slot])
                    ch_in.pre(nc.sync)
                    ch_in.dma(nc.sync.dma_start(
                        out=qbufs[slot][:, :, :],
                        in_=qt_dr.ap()[b_o2].rearrange("h d t -> d h t")))
                    q_ready[slot] = ch_in.n

                ld_q(0)
                v_evac2 = [0, 0]
                psABC = (ps_A, ps_B, ps_C)
                psDE = (ps_D, ps_E)
                exp_on_slot3 = [0, 0, 0]
                for i, (b_o, hf) in enumerate(items):
                    slot = i % 2
                    qslot = b_o % 2
                    if i + 1 < len(items):
                        ld_kv(i + 1)
                    if hf == 0 and b_o + 1 < B:
                        ld_q(b_o + 1)
                    nc.tensor.wait_ge(ch_in.sem, q_ready[qslot])
                    nc.tensor.wait_ge(ch_ing.sem, kv_ready[i])
                    V.wait(nc.tensor)
                    for hh in range(8):
                        h = 8 * hf + hh
                        hp = h % 2
                        for grp in range(4):
                            gs = grp % 3
                            pab = psABC[gs]
                            nc.tensor.wait_ge(A.sem, exp_on_slot3[gs])
                            for tq in range(4):
                                tc = 4 * grp + tq
                                ins = nc.tensor.matmul(
                                    pab[:, 256 * tq:256 * (tq + 1)],
                                    kbufs[slot][:, hh, 128 * tc:128 * (tc + 1)],
                                    qbufs[qslot][:, h, :],
                                    start=True, stop=True)
                            ch_pe.inc(ins)
                            ch_pe.wait(nc.scalar)
                            A.activation(
                                out=pTs[hp][:, 4 * grp:4 * grp + 4, :],
                                in_=pab[:, :].rearrange("p (c f) -> p c f",
                                                        f=256),
                                func=AF.Exp, scale=1.0 / 64.0)
                            exp_on_slot3[gs] = A.n
                        a_pv = A.n
                        nc.tensor.wait_ge(A.sem, a_pv)
                        nc.tensor.wait_ge(V.sem, v_evac2[hp])
                        pde = psDE[hp]
                        for sc in range(2):
                            for tc in range(16):
                                ins = nc.tensor.matmul(
                                    pde[:, 65 * sc:65 * sc + 65],
                                    pTs[hp][:, tc, 128 * sc:128 * sc + 128],
                                    vbufs[slot][:, tc, hh, :],
                                    start=(tc == 0), stop=(tc == 15))
                            ch_pe.inc(ins)
                        ch_pe.wait(nc.vector)
                        for sc in range(2):
                            V.reciprocal(out=recb[:, sc:sc + 1],
                                         in_=pde[:, 65 * sc + 64:65 * sc + 65])
                            V.tensor_scalar(
                                out=a32[:, sc, 64 * h:64 * h + 64],
                                in0=pde[:, 65 * sc:65 * sc + 64],
                                scalar1=recb[:, sc:sc + 1], scalar2=None,
                                op0=ALU.mult)
                        v_evac2[hp] = V.n
                    q_users[qslot] = ch_pe.n
                    if hf == 1:
                        for sc in range(2):
                            s0 = 128 * sc
                            V.wait(nc.tensor)
                            A.wait(nc.tensor)
                            for rnd in range(2):
                                for c4 in range(4):
                                    c = 4 * rnd + c4
                                    ins = nc.tensor.transpose(
                                        ps_E[:, 128 * c4:128 * (c4 + 1)],
                                        a32[:, sc, 128 * c:128 * (c + 1)],
                                        identf[:, :])
                                ch_pe.inc(ins)
                                ch_pe.wait(nc.vector)
                                V.tensor_copy(
                                    out=atT2[:, 4 * rnd:4 * rnd + 4, :],
                                    in_=ps_E[:, :].rearrange(
                                        "p (c f) -> p c f", f=128))
                                V.wait(nc.tensor)
                            for half in range(2):
                                for c in range(8):
                                    ins = nc.tensor.matmul(
                                        ps_C[:, 512 * half:512 * half + 512],
                                        atT2[:, c, :],
                                        wo_nat[:, c, 512 * half:512 * half + 512],
                                        start=(c == 0), stop=(c == 7))
                                ch_pe.inc(ins)
                            V.wait(nc.sync)
                            ch_in.pre(nc.sync)
                            ch_in.dma(nc.sync.dma_start(
                                out=xg[:, :],
                                in_=x_in.ap()[b_o, s0:s0 + 128, :]))
                            ch_in.pre(nc.sync)
                            ch_in.dma(nc.sync.dma_start(
                                out=x1g[:, :],
                                in_=x1_dr.ap()[s0:s0 + 128, b_o, :]))
                            ch_in.pre(nc.sync)
                            ch_in.dma(nc.sync.dma_start(
                                out=yg[:, :],
                                in_=y_dr.ap().rearrange(
                                    "(a b) e -> a b e", b=B)
                                [s0:s0 + 128, b_o, :]))
                            ch_in.wait(nc.vector)
                            ch_pe.wait(nc.vector)
                            ch_res.wait(nc.vector)
                            V.tensor_tensor(out=outsb[:, :], in0=ps_C[:, :],
                                            in1=xg[:, :], op=ALU.add)
                            V.tensor_tensor(out=outsb[:, :], in0=outsb[:, :],
                                            in1=x1g[:, :], op=ALU.add)
                            V.tensor_tensor(out=outsb[:, :], in0=outsb[:, :],
                                            in1=yg[:, :], op=ALU.add)
                            V.wait(nc.sync)
                            ch_res.pre(nc.sync)
                            ch_res.dma(nc.sync.dma_start(
                                out=out_dr.ap()[b_o, s0:s0 + 128, :],
                                in_=outsb[:, :]))
                ch_res.wait(nc.sync)
                fence()
                ph4.close()
            ALLCH.append(ch_cp)
            fence()
    return nc


# version 10 (final)
# speedup vs baseline: 1.0589x; 1.0589x over previous
import numpy as np
import ml_dtypes
from contextlib import ExitStack

import concourse.bass as bass
import concourse.mybir as mybir
from concourse.bass_utils import run_bass_kernel_spmd

F32 = mybir.dt.float32
BF16 = mybir.dt.bfloat16
AF = mybir.ActivationFunctionType
ALU = mybir.AluOpType
AX = mybir.AxisListType

NCORES = 8
B = 32
T = 2048
E = 1024
H = 16
D = 64
SLOC = T // NCORES
FF = 2048
NEXP = 8
EPS = 1e-5
BFNP = np.dtype(ml_dtypes.bfloat16)


class Ch:

    def __init__(self, nc, name):
        self.sem = nc.alloc_semaphore(name)
        self.n = 0

    def inc(self, instr, k=1):
        instr.then_inc(self.sem, k)
        self.n += k
        return instr

    def dma(self, instr):
        return self.inc(instr, 16)

    def pre(self, eng):
        eng.wait_ge(self.sem, self.n)

    def wait(self, eng):
        eng.wait_ge(self.sem, self.n)


class SE:

    def __init__(self, nc, raw, name):
        object.__setattr__(self, "raw", raw)
        object.__setattr__(self, "sem", nc.alloc_semaphore(name))
        object.__setattr__(self, "n", 0)

    def __getattr__(self, opname):
        raw_fn = getattr(self.raw, opname)

        def emit(*a, **kw):
            self.raw.wait_ge(self.sem, self.n)
            ins = raw_fn(*a, **kw)
            ins.then_inc(self.sem, 1)
            object.__setattr__(self, "n", self.n + 1)
            return ins
        return emit

    def wait(self, other_raw_eng):
        other_raw_eng.wait_ge(self.sem, self.n)


def consts_np():
    ident = np.eye(128, dtype=BFNP)
    c64 = np.eye(64, dtype=np.float32) - np.full((64, 64), 1.0 / 64,
                                                 dtype=np.float32)
    cblk = np.zeros((128, 128), dtype=np.float32)
    cblk[0:64, 0:64] = c64
    cblk[64:128, 64:128] = c64
    onesb = np.zeros((128, 2), dtype=np.float32)
    onesb[0:64, 0] = 1.0
    onesb[64:128, 1] = 1.0
    return ident, cblk.astype(BFNP), onesb.astype(BFNP)


def ln_stats(nc, V, A, x_f32, mu, r, work, junk):
    V.wait(nc.scalar)
    V.tensor_reduce(out=work[:, 0:1], in_=x_f32, axis=AX.X, op=ALU.add)
    A.activation(out=junk[:, :], in_=x_f32, func=AF.Square,
                 accum_out=work[:, 1:2])
    A.wait(nc.vector)
    V.tensor_scalar(out=mu[:, 0:1], in0=work[:, 0:1], scalar1=1.0 / E,
                    scalar2=None, op0=ALU.mult)
    V.tensor_scalar(out=work[:, 2:3], in0=work[:, 1:2], scalar1=1.0 / E,
                    scalar2=None, op0=ALU.mult)
    V.tensor_tensor(out=work[:, 3:4], in0=mu[:, 0:1], in1=mu[:, 0:1],
                    op=ALU.mult)
    V.tensor_tensor(out=work[:, 4:5], in0=work[:, 2:3], in1=work[:, 3:4],
                    op=ALU.subtract)
    V.tensor_scalar(out=work[:, 5:6], in0=work[:, 4:5], scalar1=EPS,
                    scalar2=None, op0=ALU.add)
    V.wait(nc.scalar)
    A.activation(out=work[:, 6:7], in_=work[:, 5:6], func=AF.Sqrt)
    A.wait(nc.vector)
    V.reciprocal(out=r[:, 0:1], in_=work[:, 6:7])


def build_l1(n_tiles=None):
    import os
    if n_tiles is None:
        n_tiles = int(os.environ.get("L1_TILES", "999"))
    stage = int(os.environ.get("L1_STAGE", "99"))
    nc = bass.Bass("TRN2", target_bir_lowering=False)
    x_in = nc.dram_tensor("x", [B, SLOC, E], F32, kind="ExternalInput")
    wk_in = nc.dram_tensor("wk", [E, E], F32, kind="ExternalInput")
    wv_in = nc.dram_tensor("wv", [E, E], F32, kind="ExternalInput")
    kct_out = nc.dram_tensor("kct", [B, H, 65, SLOC], BF16,
                             kind="ExternalOutput")
    v_out = nc.dram_tensor("vv", [B, H, SLOC, D], BF16, kind="ExternalOutput")

    ident_np, cblk_np, onesb_np = consts_np()
    ident_dr = nc.inline_tensor(ident_np, name="identc")
    cblk_dr = nc.inline_tensor(cblk_np, name="cblkc")
    onesb_dr = nc.inline_tensor(onesb_np, name="onesbc")

    ident = nc.alloc_sbuf_tensor("identsb", [128, 128], BF16)
    cblk = nc.alloc_sbuf_tensor("cblksb", [128, 128], BF16)
    onesb = nc.alloc_sbuf_tensor("onesbsb", [128, 2], BF16)
    wkn = nc.alloc_sbuf_tensor("wkn", [128, 8, E], BF16)
    wkT = nc.alloc_sbuf_tensor("wkT", [128, 8, E], BF16)
    wkc = nc.alloc_sbuf_tensor("wkc", [128, 8, E], BF16)
    wv = nc.alloc_sbuf_tensor("wvsb", [128, 8, E], BF16)
    xt = nc.alloc_sbuf_tensor("xt", [128, E], F32)
    mu = nc.alloc_sbuf_tensor("mu", [128, 1], F32)
    r = nc.alloc_sbuf_tensor("r", [128, 1], F32)
    work = nc.alloc_sbuf_tensor("work", [128, 8], F32)
    junk = nc.alloc_sbuf_tensor("junk", [128, E], BF16)
    xn32 = nc.alloc_sbuf_tensor("xn32", [128, E], F32)
    xnbf = nc.alloc_sbuf_tensor("xnbf", [128, E], BF16)
    xnT = nc.alloc_sbuf_tensor("xnT", [128, 8, 128], BF16)
    kcbf = nc.alloc_sbuf_tensor("kcbf", [128, 8, 128], BF16)
    sqbf = nc.alloc_sbuf_tensor("sqbf", [128, 8, 128], BF16)
    kvarT = nc.alloc_sbuf_tensor("kvarT", [2, 8, 128], BF16)
    vbf = nc.alloc_sbuf_tensor("vbf", [128, E], BF16)
    ps_t = nc.alloc_psum_tensor("ps_t", [128, 8, 128], BF16)
    ps_k = nc.alloc_psum_tensor("ps_k", [128, 8, 128], F32)
    ps_v = nc.alloc_psum_tensor("ps_v", [128, E], F32)
    ps_w = nc.alloc_psum_tensor("ps_w", [2, 8, 128], F32)

    ch_in = Ch(nc, "ch_in")
    ch_ing = Ch(nc, "ch_ing")
    ch_pe = Ch(nc, "ch_pe")
    ch_out = Ch(nc, "ch_out")

    with ExitStack() as st:
        block = st.enter_context(nc.Block())

        @block.sync
        def _(sync):
            V = SE(nc, nc.vector, "se_dve")
            A = SE(nc, nc.scalar, "se_act")

            ch_in.dma(nc.sync.dma_start(out=ident[:, :], in_=ident_dr.ap()))
            ch_in.pre(nc.sync)
            ch_in.dma(nc.sync.dma_start(out=cblk[:, :], in_=cblk_dr.ap()))
            ch_in.pre(nc.sync)
            ch_in.dma(nc.sync.dma_start(out=onesb[:, :], in_=onesb_dr.ap()))
            ch_ing.dma(nc.gpsimd.dma_start(
                out=wkn[:, :, :],
                in_=wk_in.ap().rearrange("(c p) d -> p c d", p=128)))
            ch_ing.pre(nc.gpsimd)
            ch_ing.dma(nc.gpsimd.dma_start(
                out=wv[:, :, :],
                in_=wv_in.ap().rearrange("(c p) d -> p c d", p=128)))
            ch_in.wait(nc.tensor)
            ch_ing.wait(nc.tensor)
            for cj in range(8):
                for ci in range(8):
                    ins = nc.tensor.transpose(
                        ps_t[:, ci, :], wkn[:, ci, 128 * cj:128 * (cj + 1)],
                        ident[:, :])
                ch_pe.inc(ins)
                ch_pe.wait(nc.vector)
                V.tensor_copy(out=wkT[:, cj, :],
                              in_=ps_t[:, :, :].rearrange("p c f -> p (c f)"))
                V.wait(nc.tensor)
            for ec in range(8):
                for g in range(8):
                    ins = nc.tensor.matmul(
                        ps_k[:, g, :], wkT[:, g, 128 * ec:128 * (ec + 1)],
                        cblk[:, :], start=True, stop=True)
                ch_pe.inc(ins)
                ch_pe.wait(nc.vector)
                V.tensor_copy(out=wkc[:, ec, :],
                              in_=ps_k[:, :, :].rearrange("p c f -> p (c f)"))
                V.wait(nc.tensor)

            tcount = 0
            for b_o in range(B):
                for sc in range(SLOC // 128):
                    if tcount >= n_tiles:
                        continue
                    tcount += 1
                    s0 = 128 * sc
                    V.wait(nc.sync)
                    A.wait(nc.sync)
                    ch_out.wait(nc.vector)
                    ch_in.pre(nc.sync)
                    ch_in.dma(nc.sync.dma_start(
                        out=xt[:, :], in_=x_in.ap()[b_o, s0:s0 + 128, :]))
                    ch_in.wait(nc.vector)
                    ch_in.wait(nc.scalar)
                    if stage < 1:
                        continue
                    ln_stats(nc, V, A, xt[:, :], mu, r, work, junk)
                    V.tensor_scalar(out=xn32[:, :], in0=xt[:, :],
                                    scalar1=mu[:, 0:1], scalar2=r[:, 0:1],
                                    op0=ALU.subtract, op1=ALU.mult)
                    V.tensor_copy(out=xnbf[:, :], in_=xn32[:, :])
                    V.wait(nc.tensor)
                    if stage < 2:
                        continue
                    for c in range(8):
                        ins = nc.tensor.transpose(
                            ps_t[:, c, :], xnbf[:, 128 * c:128 * (c + 1)],
                            ident[:, :])
                    ch_pe.inc(ins)
                    ch_pe.wait(nc.vector)
                    V.tensor_copy(out=xnT[:, :, :], in_=ps_t[:, :, :])
                    V.wait(nc.tensor)
                    if stage < 3:
                        continue
                    for dc in range(8):
                        for ec in range(8):
                            ins = nc.tensor.matmul(
                                ps_k[:, dc, :],
                                wkc[:, ec, 128 * dc:128 * (dc + 1)],
                                xnT[:, ec, :],
                                start=(ec == 0), stop=(ec == 7))
                        ch_pe.inc(ins)
                    ch_pe.wait(nc.scalar)
                    ch_pe.wait(nc.vector)
                    V.tensor_copy(out=kcbf[:, 0:4, :], in_=ps_k[:, 0:4, :])
                    ndve0 = V.n
                    V.tensor_copy(out=kcbf[:, 4:8, :], in_=ps_k[:, 4:8, :])
                    ndve1 = V.n
                    nc.scalar.wait_ge(V.sem, ndve0)
                    A.activation(out=sqbf[:, 0:4, :], in_=ps_k[:, 0:4, :],
                                 func=AF.Square)
                    nc.scalar.wait_ge(V.sem, ndve1)
                    A.activation(out=sqbf[:, 4:8, :], in_=ps_k[:, 4:8, :],
                                 func=AF.Square)
                    V.wait(nc.tensor)
                    A.wait(nc.tensor)
                    if stage < 4:
                        continue
                    for c in range(8):
                        ins = nc.tensor.matmul(ps_w[:, c, :], onesb[:, :],
                                               sqbf[:, c, :],
                                               start=True, stop=True)
                    ch_pe.inc(ins)
                    ch_pe.wait(nc.vector)
                    V.tensor_scalar(out=kvarT[:, :, :], in0=ps_w[:, :, :],
                                    scalar1=1.0 / 63.0, scalar2=None,
                                    op0=ALU.mult)
                    if stage < 5:
                        continue
                    for nh in range(2):
                        for ec in range(8):
                            ins = nc.tensor.matmul(
                                ps_v[:, 512 * nh:512 * (nh + 1)],
                                xnT[:, ec, :],
                                wv[:, ec, 512 * nh:512 * (nh + 1)],
                                start=(ec == 0), stop=(ec == 7))
                        ch_pe.inc(ins)
                    ch_pe.wait(nc.vector)
                    V.tensor_copy(out=vbf[:, :], in_=ps_v[:, :])
                    if stage < 6:
                        continue
                    V.wait(nc.sync)
                    kv_view = kct_out.ap()[b_o]
                    for c in range(8):
                        for j in range(2):
                            dst = kv_view[2 * c + j, 0:64, s0:s0 + 128]
                            ch_out.pre(nc.sync)
                            ch_out.dma(nc.sync.dma_start(
                                out=dst, in_=kcbf[64 * j:64 * j + 64, c, :]))
                    dstv = kv_view[:, 64, s0:s0 + 128].rearrange(
                        "(c p) t -> p c t", p=2)
                    ch_out.pre(nc.sync)
                    ch_out.dma(nc.sync.dma_start(out=dstv, in_=kvarT[:, :, :]))
                    for hh in range(2):
                        dst = v_out.ap()[b_o, 8 * hh:8 * hh + 8,
                                         s0:s0 + 128, :]
                        ch_out.pre(nc.sync)
                        ch_out.dma(nc.sync.dma_start(
                            out=dst.rearrange("h t d -> t h d"),
                            in_=vbf[:, :].rearrange("t (h d) -> t h d", d=D)
                            [:, 8 * hh:8 * hh + 8, :]))
            ch_out.wait(nc.sync)
    return nc


def run_l1(x, wk, wv):
    nc = build_l1()
    in_maps = [{
        "x": np.ascontiguousarray(x[:, SLOC * c:SLOC * (c + 1), :]),
        "wk": wk, "wv": wv,
    } for c in range(NCORES)]
    res = run_bass_kernel_spmd(nc, in_maps, core_ids=list(range(NCORES)))
    kct = np.concatenate([res.results[c]["kct"] for c in range(NCORES)],
                         axis=3)
    vv = np.concatenate([res.results[c]["vv"] for c in range(NCORES)], axis=2)
    return kct, vv


def make_in_maps(x, w):
    in_maps = []
    for c in range(NCORES):
        in_maps.append({
            "x": np.ascontiguousarray(x[:, SLOC * c:SLOC * (c + 1), :]),
            "xh": np.ascontiguousarray(
                x[:, (T // 2) * (c % 2):(T // 2) * (c % 2 + 1), :]),
            "wk": w["k"], "wv": w["v"],
            "wq": w["q"], "wo": w["o"], "ipw": w["ipw"], "ipb": w["ipb"],
            "l1w": w["l1w"], "l1b": w["l1b"], "opw": w["opw"],
            "opb": w["opb"], "l2w": w["l2w"], "l2b": w["l2b"],
            "rw": w["rw"], "w1": w["w1"], "w2": w["w2"],
        })
    return in_maps


def weights_of(inputs):
    f = lambda k: np.ascontiguousarray(np.asarray(inputs[k], dtype=np.float32))
    return f("L0_attn_x"), {
        "k": f("L0_attn_Wk"), "v": f("L0_attn_Wv"),
        "q": f("L0_attn_Wq"), "o": f("L0_attn_Wo"),
        "ipw": f("L0_ffn_in_proj_w"), "ipb": f("L0_ffn_in_proj_b"),
        "l1w": f("L0_ffn_ln1_w"), "l1b": f("L0_ffn_ln1_b"),
        "opw": f("L0_ffn_out_proj_w"), "opb": f("L0_ffn_out_proj_b"),
        "l2w": f("L0_ffn_ln2_w"), "l2b": f("L0_ffn_ln2_b"),
        "rw": f("L0_ffn_router_w"), "w1": f("L0_ffn_w1"),
        "w2": f("L0_ffn_w2")}


def kernel(**inputs):
    x, w = weights_of(inputs)
    nc = build_l2(stage=99)
    res = run_bass_kernel_spmd(nc, make_in_maps(x, w),
                               core_ids=list(range(NCORES)))
    return np.concatenate([res.results[c]["out"] for c in range(NCORES)],
                          axis=1)


def build_l2(stage=99):
    import os
    nc = bass.Bass("TRN2", target_bir_lowering=False,
                   detect_race_conditions=(os.environ.get("NORACE") != "1"))
    x_in = nc.dram_tensor("x", [B, SLOC, E], F32, kind="ExternalInput")
    x2_in = nc.dram_tensor("xh", [B, T // 2, E], F32, kind="ExternalInput")
    wk_in = nc.dram_tensor("wk", [E, E], F32, kind="ExternalInput")
    wv_in = nc.dram_tensor("wv", [E, E], F32, kind="ExternalInput")
    kct_sh = nc.dram_tensor("KCTsh", [B, H, 65, T], BF16, kind="Internal",
                            addr_space="Shared")
    v_sh = nc.dram_tensor("Vsh", [B, H, T, D], BF16, kind="Internal",
                          addr_space="Shared")
    flag_sh = nc.dram_tensor("FLAGsh", [2, 16], mybir.dt.uint32,
                             kind="Internal", addr_space="Shared")
    kct_pr = nc.dram_tensor("KCTpr", [B, H, 65, T // 2], BF16)
    v_pr = nc.dram_tensor("Vpr", [B, H, T // 2, D], BF16)
    wq_in = nc.dram_tensor("wq", [E, E], F32, kind="ExternalInput")
    wo_in = nc.dram_tensor("wo", [E, E], F32, kind="ExternalInput")
    ipw_in = nc.dram_tensor("ipw", [3 * E, E], F32, kind="ExternalInput")
    ipb_in = nc.dram_tensor("ipb", [3 * E], F32, kind="ExternalInput")
    l1w_in = nc.dram_tensor("l1w", [E], F32, kind="ExternalInput")
    l1b_in = nc.dram_tensor("l1b", [E], F32, kind="ExternalInput")
    opw_in = nc.dram_tensor("opw", [E, E], F32, kind="ExternalInput")
    opb_in = nc.dram_tensor("opb", [E], F32, kind="ExternalInput")
    l2w_in = nc.dram_tensor("l2w", [E], F32, kind="ExternalInput")
    l2b_in = nc.dram_tensor("l2b", [E], F32, kind="ExternalInput")
    rw_in = nc.dram_tensor("rw", [NEXP, E], F32, kind="ExternalInput")
    w1_in = nc.dram_tensor("w1", [NEXP, E, FF], F32, kind="ExternalInput")
    w2_in = nc.dram_tensor("w2", [NEXP, FF, E], F32, kind="ExternalInput")
    out_dr = nc.dram_tensor("out", [B, SLOC, E], F32, kind="ExternalOutput")

    qt_dr = nc.dram_tensor("QT", [B, H, 65, SLOC], BF16)
    qkv_dr = nc.dram_tensor("QKV", [SLOC, B, 3, H, D], BF16)
    xn_dr = nc.dram_tensor("XNs", [SLOC, B, E], F32)
    x1_dr = nc.dram_tensor("X1s", [SLOC, B, E], F32)
    hn2_dr = nc.dram_tensor("HN2T", [16, 8, 128, 512], BF16)
    y_dr = nc.dram_tensor("Ys", [SLOC * B, E], F32)

    ident_np, cblk_np, onesb_np = consts_np()
    identf_np = np.eye(128, dtype=np.float32)
    mask_np = np.zeros((128, H, 32), dtype=np.float32)
    for p in range(128):
        mask_np[p, :, 0:(p % 32) + 1] = 1.0
    ident_dr = nc.inline_tensor(ident_np, name="identc")
    identf_dr = nc.inline_tensor(identf_np, name="identfc")
    onesb_dr = nc.inline_tensor(onesb_np, name="onesbc")
    cblk_dr = nc.inline_tensor(cblk_np, name="cblkc")
    mask_dr = nc.inline_tensor(mask_np.astype(BFNP), name="maskc")

    ident = nc.alloc_sbuf_tensor("identsb", [128, 128], BF16)
    identf = nc.alloc_sbuf_tensor("identfsb", [128, 128], F32)
    onesb = nc.alloc_sbuf_tensor("onesbsb", [128, 2], BF16)
    mask = nc.alloc_sbuf_tensor("masksb", [128, H, 32], BF16)
    gates = nc.alloc_sbuf_tensor("gatessb", [128, 64 * NEXP], F32)
    ln2w = nc.alloc_sbuf_tensor("ln2w", [128, 8], F32)
    ln2b = nc.alloc_sbuf_tensor("ln2b", [128, 8], F32)
    ps_A = nc.alloc_psum_tensor("ps_A", [128, 1024], F32)
    ps_B = nc.alloc_psum_tensor("ps_B", [128, 1024], F32)
    ps_C = nc.alloc_psum_tensor("ps_C", [128, 1024], F32)
    ps_D = nc.alloc_psum_tensor("ps_D", [128, 512], F32)
    ps_E = nc.alloc_psum_tensor("ps_E", [128, 512], F32)

    ch_in = Ch(nc, "ch_in")
    ch_ing = Ch(nc, "ch_ing")
    ch_pe = Ch(nc, "ch_pe")
    ch_st = Ch(nc, "ch_st")
    ch_res = Ch(nc, "ch_res")

    ALLCH = [ch_in, ch_ing, ch_pe, ch_st, ch_res]

    with ExitStack() as st:
        block = st.enter_context(nc.Block())

        @block.sync
        def _(sync):
            V = SE(nc, nc.vector, "se_dve")
            A = SE(nc, nc.scalar, "se_act")

            def fence():
                for eng in (nc.tensor, nc.vector, nc.scalar, nc.sync,
                            nc.gpsimd):
                    for ch in ALLCH:
                        ch.wait(eng)
                    V.wait(eng)
                    A.wait(eng)

            def ld_sync(dst, src):
                ch_in.pre(nc.sync)
                ch_in.dma(nc.sync.dma_start(out=dst, in_=src))

            def ld_g(dst, src):
                ch_ing.pre(nc.gpsimd)
                ch_ing.dma(nc.gpsimd.dma_start(out=dst, in_=src))

            def st_sync(dst, src):
                ch_st.pre(nc.sync)
                ch_st.dma(nc.sync.dma_start(out=dst, in_=src))

            def transpose_f32(dst_bf, src_f32, n, pst):
                V.wait(nc.tensor)
                A.wait(nc.tensor)
                for c in range(n):
                    ins = nc.tensor.transpose(
                        pst[:, 128 * c:128 * (c + 1)],
                        src_f32[:, 128 * c:128 * (c + 1)], identf[:, :])
                ch_pe.inc(ins)
                ch_pe.wait(nc.vector)
                V.tensor_copy(
                    out=dst_bf[:, 0:n, :],
                    in_=pst[:, 0:128 * n].rearrange("p (c f) -> p c f", f=128))
                V.wait(nc.tensor)

            ld_sync(ident[:, :], ident_dr.ap())
            ld_sync(identf[:, :], identf_dr.ap())
            ld_sync(onesb[:, :], onesb_dr.ap())
            ld_sync(mask[:, :, :], mask_dr.ap())

            vtmp8 = nc.alloc_sbuf_tensor("vtmp8", [8, 128], F32)

            def load_col(dst_col, vec_ap):
                ch_pe.wait(nc.sync)
                ld_sync(vtmp8[:, :], vec_ap.rearrange("(c p) -> c p", p=128))
                ch_in.wait(nc.tensor)
                V.wait(nc.tensor)
                ins = nc.tensor.transpose(ps_E[:, 0:8], vtmp8[:, :],
                                          identf[0:8, 0:8])
                ch_pe.inc(ins)
                ch_pe.wait(nc.vector)
                V.tensor_copy(out=dst_col, in_=ps_E[:, 0:8])
                V.wait(nc.tensor)

            load_col(ln2w[:, :], l2w_in.ap())
            load_col(ln2b[:, :], l2b_in.ap())

            ipbrow = nc.alloc_sbuf_tensor("ipbrow", [1, 3 * E], F32)
            fbz = nc.alloc_sbuf_tensor("fbz", [2, 16], mybir.dt.uint32)
            fone = nc.alloc_sbuf_tensor("fone", [1, 16], mybir.dt.uint32)
            tmp32 = nc.alloc_sbuf_tensor("tmp32", [128, E], F32)
            xt = nc.alloc_sbuf_tensor("xt", [128, E], F32)
            mu = nc.alloc_sbuf_tensor("mu", [128, 1], F32)
            rr = nc.alloc_sbuf_tensor("rr", [128, 1], F32)
            r2 = nc.alloc_sbuf_tensor("r2", [128, 1], F32)
            work = nc.alloc_sbuf_tensor("work", [128, 16], F32)
            junk = nc.alloc_sbuf_tensor("junk", [128, E], BF16)
            ph1 = ExitStack()
            wq = ph1.enter_context(nc.sbuf_tensor("wqsb", [128, 8, E], BF16))
            ipT = ph1.enter_context(nc.sbuf_tensor("ipT", [128, 8, 3 * E], BF16))
            brep = ph1.enter_context(nc.sbuf_tensor("brep", [128, 3 * E], F32))
            ln1w = ph1.enter_context(nc.sbuf_tensor("ln1w", [128, 8], F32))
            ln1bb = ph1.enter_context(nc.sbuf_tensor("ln1bb", [128, 8], BF16))
            xn32 = ph1.enter_context(nc.sbuf_tensor("xn32", [128, E], F32))
            xnT = ph1.enter_context(nc.sbuf_tensor("xnT", [128, 8, 128], BF16))
            qtb = ph1.enter_context(nc.sbuf_tensor("qtb", [128, 8, 128], BF16))
            sqb = ph1.enter_context(nc.sbuf_tensor("sqb", [128, 8, 128], BF16))
            stq = ph1.enter_context(nc.sbuf_tensor("stq", [2, 8, 128], F32))
            sts = ph1.enter_context(nc.sbuf_tensor("sts", [2, 8, 128], F32))
            qvarT = ph1.enter_context(nc.sbuf_tensor("qvarT", [2, 8, 128], BF16))
            st3 = ph1.enter_context(nc.sbuf_tensor("st3", [2, 8, 128], F32))
            qkvb = ph1.enter_context(nc.sbuf_tensor("qkvb", [128, 3 * E], BF16))

            ld_g(wq[:, :, :], wq_in.ap().rearrange("(c p) d -> p c d", p=128))
            load_col(ln1w[:, :], l1w_in.ap())
            for i in range(24):
                ch_pe.wait(nc.sync)
                ld_sync(tmp32[:, :], ipw_in.ap()[128 * i:128 * (i + 1), :])
                ch_in.wait(nc.tensor)
                V.wait(nc.tensor)
                for j in range(8):
                    ins = nc.tensor.transpose(
                        ps_C[:, 512 * (j % 2):512 * (j % 2) + 128],
                        tmp32[:, 128 * j:128 * (j + 1)], identf[:, :])
                    ch_pe.inc(ins)
                    ch_pe.wait(nc.vector)
                    V.tensor_copy(out=ipT[:, j, 128 * i:128 * (i + 1)],
                                  in_=ps_C[:, 512 * (j % 2):512 * (j % 2) + 128])
                    V.wait(nc.tensor)
            ln1b_tmp = ph1.enter_context(nc.sbuf_tensor("ln1btmp", [128, 8], F32))
            load_col(ln1b_tmp[:, :], l1b_in.ap())
            V.tensor_copy(out=ln1bb[:, :], in_=ln1b_tmp[:, :])
            V.wait(nc.tensor)
            for n6 in range(6):
                for ec in range(8):
                    ins = nc.tensor.matmul(
                        ps_D[0:1, 0:512], ln1bb[:, ec:ec + 1],
                        ipT[:, ec, 512 * n6:512 * (n6 + 1)],
                        start=(ec == 0), stop=(ec == 7))
                ch_pe.inc(ins)
                ch_pe.wait(nc.vector)
                V.tensor_copy(out=brep[0:1, 512 * n6:512 * (n6 + 1)],
                              in_=ps_D[0:1, 0:512])
                V.wait(nc.tensor)
            ld_sync(ipbrow[:, :], ipb_in.ap()[None, :])
            ch_in.wait(nc.vector)
            V.tensor_tensor(out=brep[0:1, :], in0=brep[0:1, :],
                            in1=ipbrow[0:1, :], op=ALU.add)
            V.wait(nc.sync)
            rep = 1
            while rep < 128:
                ch_st.pre(nc.sync)
                ch_st.dma(nc.sync.dma_start(
                    out=brep[rep:2 * rep, :], in_=brep[0:rep, :]))
                rep *= 2
            ch_st.wait(nc.vector)
            for j in range(8):
                V.tensor_scalar(out=ipT[:, j, :], in0=ipT[:, j, :],
                                scalar1=ln1w[:, j:j + 1], scalar2=None,
                                op0=ALU.mult)


            V.memset(fbz[:, :], 0)
            V.memset(fone[:, :], 1)
            ch_stg = Ch(nc, "ch_stg")
            ch_sts = Ch(nc, "ch_sts")
            ALLCH.append(ch_sts)
            nullsem = nc.alloc_semaphore("nullsem")
            nullsem2 = nc.alloc_semaphore("nullsem2")
            import os as _os

            ph0 = ExitStack()
            wkc0 = ph0.enter_context(nc.sbuf_tensor("wkc0", [128, 8, E], BF16))
            wv0 = ph0.enter_context(nc.sbuf_tensor("wv0", [128, 8, E], BF16))
            wkT0 = ph0.enter_context(nc.sbuf_tensor("wkT0", [128, 8, E], BF16))
            cblk0 = ph0.enter_context(nc.sbuf_tensor("cblk0", [128, 128], BF16))
            xn32k = ph0.enter_context(nc.sbuf_tensor("xn32k", [128, E], F32))
            xnTk = ph0.enter_context(nc.sbuf_tensor("xnTk", [128, 8, 128], BF16))
            kcb0 = ph0.enter_context(nc.sbuf_tensor("kcb0", [128, 8, 128], BF16))
            sqb0 = ph0.enter_context(nc.sbuf_tensor("sqb0", [128, 8, 128], BF16))
            kvarT0 = ph0.enter_context(nc.sbuf_tensor("kvarT0", [2, 8, 128], BF16))
            vbf0 = ph0.enter_context(nc.sbuf_tensor("vbf0", [128, E], BF16))
            ld_sync(cblk0[:, :], cblk_dr.ap())
            ld_g(wv0[:, :, :], wv_in.ap().rearrange("(c p) d -> p c d", p=128))
            for i in range(8):
                ch_pe.wait(nc.sync)
                ld_sync(tmp32[:, :], wk_in.ap()[128 * i:128 * (i + 1), :])
                ch_in.wait(nc.tensor)
                V.wait(nc.tensor)
                for j in range(8):
                    ins = nc.tensor.transpose(
                        ps_C[:, 512 * (j % 2):512 * (j % 2) + 128],
                        tmp32[:, 128 * j:128 * (j + 1)], identf[:, :])
                    ch_pe.inc(ins)
                    ch_pe.wait(nc.vector)
                    V.tensor_copy(out=wkT0[:, j, 128 * i:128 * (i + 1)],
                                  in_=ps_C[:, 512 * (j % 2):512 * (j % 2) + 128])
                    V.wait(nc.tensor)
            ch_in.wait(nc.tensor)
            for ec in range(8):
                for g8 in range(8):
                    ins = nc.tensor.matmul(
                        ps_A[:, 128 * g8:128 * (g8 + 1)],
                        wkT0[:, g8, 128 * ec:128 * (ec + 1)],
                        cblk0[:, :], start=True, stop=True)
                ch_pe.inc(ins)
                ch_pe.wait(nc.vector)
                V.tensor_copy(out=wkc0[:, ec, :],
                              in_=ps_A[:, :].rearrange("p (c f) -> p c f",
                                                       f=128))
                V.wait(nc.tensor)
            ch_ing.wait(nc.tensor)
            _kvpart = _os.environ.get("KVPART", "")
            if _os.environ.get("DSPROBE") == "2":
                ch_stg.dma(nc.gpsimd.dma_start(
                    out=kct_sh.ap()[0][0, 0:64, bass.ds(tb_sv + 0, 128)],
                    in_=junk[0:64, 0:128]))
                print("POST-PREP DS OK")
            xt_s = [xt, ph0.enter_context(nc.sbuf_tensor("xt_b", [128, E], F32))]
            kcb_s = [kcb0, ph0.enter_context(nc.sbuf_tensor("kcb0_b", [128, 8, 128], BF16))]
            sqb_s = [sqb0, ph0.enter_context(nc.sbuf_tensor("sqb0_b", [128, 8, 128], BF16))]
            kvr_s = [kvarT0, ph0.enter_context(nc.sbuf_tensor("kvarT0_b", [2, 8, 128], BF16))]
            vbf_s = [vbf0, ph0.enter_context(nc.sbuf_tensor("vbf0_b", [128, E], BF16))]
            kv_tiles = [(b_o, 128 * sck) for b_o in range(B if _kvpart != "1" else 0)
                        for sck in range(8)]
            x_ready = [None, None]
            cons_v = [0, 0]
            cons_a = [0, 0]
            stg_cnt = [0, 0]
            sts_cnt = [0, 0]

            def ld_x_kv(i):
                slot = i % 2
                b_o, s0k = kv_tiles[i]
                nc.sync.wait_ge(V.sem, cons_v[slot])
                nc.sync.wait_ge(A.sem, cons_a[slot])
                ch_in.pre(nc.sync)
                ch_in.dma(nc.sync.dma_start(
                    out=xt_s[slot][:, :],
                    in_=x2_in.ap()[b_o, s0k:s0k + 128, :]))
                x_ready[slot] = ch_in.n

            if kv_tiles:
                ld_x_kv(0)
            for it, (b_o, s0k) in enumerate(kv_tiles):
                slot = it % 2
                if it + 1 < len(kv_tiles):
                    ld_x_kv(it + 1)
                nc.vector.wait_ge(ch_in.sem, x_ready[slot])
                nc.scalar.wait_ge(ch_in.sem, x_ready[slot])
                nc.vector.wait_ge(ch_stg.sem, stg_cnt[slot])
                nc.scalar.wait_ge(ch_stg.sem, stg_cnt[slot])
                nc.vector.wait_ge(ch_sts.sem, sts_cnt[slot])
                nc.scalar.wait_ge(ch_sts.sem, sts_cnt[slot])
                ln_stats(nc, V, A, xt_s[slot][:, :], mu, rr, work, junk)
                V.tensor_scalar(out=xn32k[:, :], in0=xt_s[slot][:, :],
                                scalar1=mu[:, 0:1], scalar2=rr[:, 0:1],
                                op0=ALU.subtract, op1=ALU.mult)
                transpose_f32(xnTk, xn32k, 8, ps_C)
                for dc in range(8):
                    for ec in range(8):
                        ins = nc.tensor.matmul(
                            ps_A[:, 128 * dc:128 * (dc + 1)],
                            wkc0[:, ec, 128 * dc:128 * (dc + 1)],
                            xnTk[:, ec, :],
                            start=(ec == 0), stop=(ec == 7))
                    ch_pe.inc(ins)
                pe_kc = ch_pe.n
                for nh in range(2):
                    for ec in range(8):
                        ins = nc.tensor.matmul(
                            ps_B[:, 512 * nh:512 * (nh + 1)],
                            xnTk[:, ec, :],
                            wv0[:, ec, 512 * nh:512 * (nh + 1)],
                            start=(ec == 0), stop=(ec == 7))
                    ch_pe.inc(ins)
                pe_v = ch_pe.n
                nc.vector.wait_ge(ch_pe.sem, pe_kc)
                nc.scalar.wait_ge(ch_pe.sem, pe_kc)
                V.tensor_copy(out=kcb_s[slot][:, 0:4, :],
                              in_=ps_A[:, 0:512].rearrange(
                                  "p (c f) -> p c f", f=128))
                nv0 = V.n
                V.tensor_copy(out=kcb_s[slot][:, 4:8, :],
                              in_=ps_A[:, 512:1024].rearrange(
                                  "p (c f) -> p c f", f=128))
                nv1 = V.n
                nc.scalar.wait_ge(V.sem, nv0)
                A.activation(out=sqb_s[slot][:, 0:4, :],
                             in_=ps_A[:, 0:512].rearrange(
                                 "p (c f) -> p c f", f=128),
                             func=AF.Square)
                nc.scalar.wait_ge(V.sem, nv1)
                A.activation(out=sqb_s[slot][:, 4:8, :],
                             in_=ps_A[:, 512:1024].rearrange(
                                 "p (c f) -> p c f", f=128),
                             func=AF.Square)
                a_sq = A.n
                nc.vector.wait_ge(ch_pe.sem, pe_v)
                V.tensor_copy(out=vbf_s[slot][:, :], in_=ps_B[:, :])
                nc.tensor.wait_ge(A.sem, a_sq)
                for c in range(8):
                    ins = nc.tensor.matmul(
                        ps_C[0:2, 128 * c:128 * (c + 1)], onesb[:, :],
                        sqb_s[slot][:, c, :], start=True, stop=True)
                ch_pe.inc(ins)
                nc.vector.wait_ge(ch_pe.sem, ch_pe.n)
                V.tensor_scalar(out=kvr_s[slot][:, :, :],
                                in0=ps_C[0:2, :].rearrange(
                                    "p (c f) -> p c f", f=128),
                                scalar1=1.0 / 63.0, scalar2=None,
                                op0=ALU.mult)
                cons_v[slot] = V.n
                cons_a[slot] = A.n
                V.wait(nc.gpsimd)
                V.wait(nc.sync)
                kvv = kct_pr.ap()[b_o]
                kvh = kvv.rearrange("(c j) s t -> j c s t", j=2)
                nc.sync.dma_start(
                    out=kvh[0][:, 0:64, s0k:s0k + 128].rearrange(
                        "c s t -> s c t"),
                    in_=kcb_s[slot][0:64, :, :]).then_inc(nullsem2, 16)
                ch_sts.dma(nc.sync.dma_start(
                    out=kvh[1][:, 0:64, s0k:s0k + 128].rearrange(
                        "c s t -> s c t"),
                    in_=kcb_s[slot][64:128, :, :]))
                sts_cnt[slot] = ch_sts.n
                nc.gpsimd.dma_start(
                    out=kvv[:, 64, s0k:s0k + 128].rearrange(
                        "(c p) t -> p c t", p=2),
                    in_=kvr_s[slot][:, :, :]).then_inc(nullsem, 16)
                nc.gpsimd.dma_start(
                    out=v_pr.ap()[b_o, 0:8, s0k:s0k + 128, :]
                    .rearrange("h t d -> t h d"),
                    in_=vbf_s[slot][:, :].rearrange("t (h d) -> t h d", d=D)
                    [:, 0:8, :]).then_inc(nullsem, 16)
                ch_stg.dma(nc.gpsimd.dma_start(
                    out=v_pr.ap()[b_o, 8:16, s0k:s0k + 128, :]
                    .rearrange("h t d -> t h d"),
                    in_=vbf_s[slot][:, :].rearrange("t (h d) -> t h d", d=D)
                    [:, 8:16, :]))
                stg_cnt[slot] = ch_stg.n
            ch_stg.pre(nc.gpsimd)
            ch_sts.wait(nc.gpsimd)
            V.wait(nc.gpsimd)
            ch_cp = Ch(nc, "ch_cp")
            for arm in (nc.switch_core_id([nc.gpsimd], n=128)
                        if _kvpart != "2" else []):
                lg = {4: 0, 5: 1, 6: 2, 7: 3, 2: 4, 3: 5, 0: 6, 1: 7}[int(arm) % 8]
                t0 = (lg % 2) * (T // 2)
                for b_o in range(B):
                    nc.gpsimd.dma_start(
                        out=kct_sh.ap()[b_o, :, :, t0:t0 + T // 2],
                        in_=kct_pr.ap()[b_o]).then_inc(ch_cp.sem, 16)
                    nc.gpsimd.dma_start(
                        out=v_sh.ap()[b_o, :, t0:t0 + T // 2, :],
                        in_=v_pr.ap()[b_o]).then_inc(ch_cp.sem, 16)
                nc.gpsimd.wait_ge(ch_cp.sem, 64 * 16)
                nc.gpsimd.dma_start(
                    out=flag_sh.ap()[lg % 2, :][None, :],
                    in_=fone[0:1, :]).then_inc(ch_cp.sem, 16)
            ch_cp.n += 65 * 16
            fence()
            ph0.close()

            ch_ing.wait(nc.tensor)
            ch_in.wait(nc.tensor)
            if stage >= 1:
                xtp_s = [ph1.enter_context(nc.sbuf_tensor(f"xtp{i}", [128, E], F32)) for i in range(2)]
                qtb_s = [qtb, ph1.enter_context(nc.sbuf_tensor("qtb_b", [128, 8, 128], BF16))]
                qvr_s = [qvarT, ph1.enter_context(nc.sbuf_tensor("qvarT_b", [2, 8, 128], BF16))]
                qkv_s = [qkvb, ph1.enter_context(nc.sbuf_tensor("qkvb_b", [128, 3 * E], BF16))]
                xn_s = [xn32, ph1.enter_context(nc.sbuf_tensor("xn32_b", [128, E], F32))]
                p1_tiles = [(b_o, 128 * sc) for b_o in range(B)
                            for sc in range(SLOC // 128)]
                x_ready1 = [None, None]
                cons_v1 = [0, 0]
                cons_a1 = [0, 0]
                st_cnt1 = [0, 0]

                def ld_x_p1(i):
                    slot = i % 2
                    b_o, s0 = p1_tiles[i]
                    nc.sync.wait_ge(V.sem, cons_v1[slot])
                    nc.sync.wait_ge(A.sem, cons_a1[slot])
                    ch_in.pre(nc.sync)
                    ch_in.dma(nc.sync.dma_start(
                        out=xtp_s[slot][:, :],
                        in_=x_in.ap()[b_o, s0:s0 + 128, :]))
                    x_ready1[slot] = ch_in.n

                ld_x_p1(0)
                for it, (b_o, s0) in enumerate(p1_tiles):
                    slot = it % 2
                    if it + 1 < len(p1_tiles):
                        ld_x_p1(it + 1)
                    nc.vector.wait_ge(ch_in.sem, x_ready1[slot])
                    nc.scalar.wait_ge(ch_in.sem, x_ready1[slot])
                    nc.vector.wait_ge(ch_st.sem, st_cnt1[slot])
                    nc.scalar.wait_ge(ch_st.sem, st_cnt1[slot])
                    xt1 = xtp_s[slot]
                    ln_stats(nc, V, A, xt1[:, :], mu, rr, work, junk)
                    V.reciprocal(out=work[:, 7:8], in_=work[:, 5:6])
                    V.tensor_tensor(out=work[:, 8:9], in0=work[:, 4:5],
                                    in1=work[:, 7:8], op=ALU.mult)
                    V.tensor_scalar(out=work[:, 9:10], in0=work[:, 8:9],
                                    scalar1=EPS, scalar2=None, op0=ALU.add)
                    V.wait(nc.scalar)
                    A.activation(out=work[:, 10:11], in_=work[:, 9:10],
                                 func=AF.Sqrt)
                    A.wait(nc.vector)
                    V.reciprocal(out=r2[:, 0:1], in_=work[:, 10:11])
                    V.tensor_scalar(out=xn_s[slot][:, :], in0=xt1[:, :],
                                    scalar1=mu[:, 0:1], scalar2=rr[:, 0:1],
                                    op0=ALU.subtract, op1=ALU.mult)
                    transpose_f32(xnT, xn_s[slot], 8, ps_C)
                    for dc in range(8):
                        for ec in range(8):
                            ins = nc.tensor.matmul(
                                ps_A[:, 128 * dc:128 * (dc + 1)],
                                wq[:, ec, 128 * dc:128 * (dc + 1)],
                                xnT[:, ec, :],
                                start=(ec == 0), stop=(ec == 7))
                        ch_pe.inc(ins)
                    pe_q = ch_pe.n
                    for nh in range(2):
                        for ec in range(8):
                            ins = nc.tensor.matmul(
                                ps_B[:, 512 * nh:512 * (nh + 1)],
                                xnT[:, ec, :],
                                ipT[:, ec, 512 * nh:512 * (nh + 1)],
                                start=(ec == 0), stop=(ec == 7))
                        ch_pe.inc(ins)
                    pe_k1 = ch_pe.n
                    nc.vector.wait_ge(ch_pe.sem, pe_q)
                    nc.scalar.wait_ge(ch_pe.sem, pe_q)
                    V.tensor_copy(out=qtb_s[slot][:, 0:4, :],
                                  in_=ps_A[:, 0:512].rearrange(
                                      "p (c f) -> p c f", f=128))
                    ndve0 = V.n
                    V.tensor_copy(out=qtb_s[slot][:, 4:8, :],
                                  in_=ps_A[:, 512:1024].rearrange(
                                      "p (c f) -> p c f", f=128))
                    ndve1 = V.n
                    nc.scalar.wait_ge(V.sem, ndve0)
                    A.activation(out=sqb[:, 0:4, :],
                                 in_=ps_A[:, 0:512].rearrange(
                                     "p (c f) -> p c f", f=128),
                                 func=AF.Square)
                    nc.scalar.wait_ge(V.sem, ndve1)
                    A.activation(out=sqb[:, 4:8, :],
                                 in_=ps_A[:, 512:1024].rearrange(
                                     "p (c f) -> p c f", f=128),
                                 func=AF.Square)
                    a_sq = A.n
                    nc.vector.wait_ge(ch_pe.sem, pe_k1)
                    V.scalar_tensor_tensor(
                        out=qkv_s[slot][:, 0:1024], in0=ps_B[:, :],
                        scalar=r2[:, 0:1], in1=brep[:, 0:1024],
                        op0=ALU.mult, op1=ALU.add)
                    v_k1 = V.n
                    for nh in range(2):
                        for ec in range(8):
                            ins = nc.tensor.matmul(
                                ps_C[:, 512 * nh:512 * (nh + 1)],
                                xnT[:, ec, :],
                                ipT[:, ec, 1024 + 512 * nh:1536 + 512 * nh],
                                start=(ec == 0), stop=(ec == 7))
                        ch_pe.inc(ins)
                    pe_k2 = ch_pe.n
                    nc.tensor.wait_ge(A.sem, a_sq)
                    for rnd in range(2):
                        for cc in range(4):
                            c = 4 * rnd + cc
                            ins = nc.tensor.matmul(
                                ps_D[0:2, 128 * cc:128 * (cc + 1)],
                                onesb[:, :], qtb_s[slot][:, c, :],
                                start=True, stop=True)
                            ins2 = nc.tensor.matmul(
                                ps_E[0:2, 128 * cc:128 * (cc + 1)],
                                onesb[:, :], sqb[:, c, :],
                                start=True, stop=True)
                        ch_pe.inc(ins2)
                        nc.vector.wait_ge(ch_pe.sem, ch_pe.n)
                        V.tensor_copy(
                            out=stq[:, 4 * rnd:4 * rnd + 4, :],
                            in_=ps_D[0:2, 0:512].rearrange(
                                "p (c f) -> p c f", f=128))
                        V.tensor_copy(
                            out=sts[:, 4 * rnd:4 * rnd + 4, :],
                            in_=ps_E[0:2, 0:512].rearrange(
                                "p (c f) -> p c f", f=128))
                        vst = V.n
                        if rnd == 0:
                            nc.tensor.wait_ge(V.sem, vst)
                    nc.vector.wait_ge(ch_pe.sem, pe_k2)
                    V.scalar_tensor_tensor(
                        out=qkv_s[slot][:, 1024:2048], in0=ps_C[:, :],
                        scalar=r2[:, 0:1], in1=brep[:, 1024:2048],
                        op0=ALU.mult, op1=ALU.add)
                    for nh in range(2):
                        for ec in range(8):
                            ins = nc.tensor.matmul(
                                ps_B[:, 512 * nh:512 * (nh + 1)],
                                xnT[:, ec, :],
                                ipT[:, ec, 2048 + 512 * nh:2560 + 512 * nh],
                                start=(ec == 0), stop=(ec == 7))
                        ch_pe.inc(ins)
                    pe_k3 = ch_pe.n
                    V.tensor_tensor(out=st3[:, :, :], in0=stq[:, :, :],
                                    in1=stq[:, :, :], op=ALU.mult)
                    V.tensor_scalar(out=st3[:, :, :], in0=st3[:, :, :],
                                    scalar1=1.0 / (64.0 * 63.0),
                                    scalar2=None, op0=ALU.mult)
                    V.tensor_scalar(out=stq[:, :, :], in0=sts[:, :, :],
                                    scalar1=1.0 / 63.0, scalar2=None,
                                    op0=ALU.mult)
                    V.tensor_tensor(out=qvr_s[slot][:, :, :],
                                    in0=stq[:, :, :],
                                    in1=st3[:, :, :], op=ALU.subtract)
                    nc.vector.wait_ge(ch_pe.sem, pe_k3)
                    V.scalar_tensor_tensor(
                        out=qkv_s[slot][:, 2048:3072], in0=ps_B[:, :],
                        scalar=r2[:, 0:1], in1=brep[:, 2048:3072],
                        op0=ALU.mult, op1=ALU.add)
                    cons_v1[slot] = V.n
                    cons_a1[slot] = A.n
                    V.wait(nc.sync)
                    qv = qt_dr.ap()[b_o]
                    qvh = qv.rearrange("(c j) s t -> j c s t", j=2)
                    nc.sync.dma_start(
                        out=qvh[0][:, 0:64, s0:s0 + 128].rearrange(
                            "c s t -> s c t"),
                        in_=qtb_s[slot][0:64, :, :]).then_inc(nullsem2, 16)
                    nc.sync.dma_start(
                        out=qvh[1][:, 0:64, s0:s0 + 128].rearrange(
                            "c s t -> s c t"),
                        in_=qtb_s[slot][64:128, :, :]).then_inc(nullsem2, 16)
                    nc.sync.dma_start(
                        out=qv[:, 64, s0:s0 + 128].rearrange(
                            "(c p) t -> p c t", p=2),
                        in_=qvr_s[slot][:, :, :]).then_inc(nullsem2, 16)
                    nc.sync.dma_start(
                        out=qkv_dr.ap()[s0:s0 + 128, b_o, :, :, :]
                        .rearrange("t q h d -> t (q h d)"),
                        in_=qkv_s[slot][:, :]).then_inc(nullsem2, 16)
                    ch_st.dma(nc.sync.dma_start(
                        out=xn_dr.ap()[s0:s0 + 128, b_o, :],
                        in_=xn_s[slot][:, :]))
                    st_cnt1[slot] = ch_st.n
            fence()
            ph1.close()
            if stage >= 2:
                ph2 = ExitStack()
                opjT = ph2.enter_context(nc.sbuf_tensor("opjT", [128, 8, E], BF16))
                opbr = ph2.enter_context(nc.sbuf_tensor("opbr", [128, E], F32))
                rwT32 = ph2.enter_context(nc.sbuf_tensor("rwT32", [128, 8, 8], F32))
                rwThi = ph2.enter_context(nc.sbuf_tensor("rwThi", [128, 8, 8], BF16))
                rwThi32 = ph2.enter_context(nc.sbuf_tensor("rwThi32", [128, 8, 8], F32))
                rwTlo = ph2.enter_context(nc.sbuf_tensor("rwTlo", [128, 8, 8], BF16))
                rw8 = ph2.enter_context(nc.sbuf_tensor("rw8", [8, E], F32))
                qib = ph2.enter_context(nc.sbuf_tensor("qib", [128, E], BF16))
                kib = ph2.enter_context(nc.sbuf_tensor("kib", [128, E], BF16))
                vib = ph2.enter_context(nc.sbuf_tensor("vib", [128, E], BF16))
                qTt = ph2.enter_context(nc.sbuf_tensor("qTt", [128, E], BF16))
                kTt = ph2.enter_context(nc.sbuf_tensor("kTt", [128, E], BF16))
                xng = ph2.enter_context(nc.sbuf_tensor("xng", [128, E], F32))
                exb = ph2.enter_context(nc.sbuf_tensor("exb", [128, H, 32], BF16))
                pmb = ph2.enter_context(nc.sbuf_tensor("pmb", [128, H, 32], BF16))
                ptb = ph2.enter_context(nc.sbuf_tensor("ptb", [128, H, 32], BF16))
                sums = ph2.enter_context(nc.sbuf_tensor("sums", [128, H], F32))
                recs = ph2.enter_context(nc.sbuf_tensor("recs", [128, H], F32))
                at32 = ph2.enter_context(nc.sbuf_tensor("at32", [128, E], F32))
                atT = ph2.enter_context(nc.sbuf_tensor("atT", [128, 8, 128], BF16))
                x1sb = ph2.enter_context(nc.sbuf_tensor("x1sb", [128, E], F32))
                z32 = ph2.enter_context(nc.sbuf_tensor("z32", [128, E], F32))
                hnT32 = ph2.enter_context(nc.sbuf_tensor("hnT32", [128, 8, 128], F32))
                hnbf = ph2.enter_context(nc.sbuf_tensor("hnbf", [128, 8, 128], BF16))
                hnlo = ph2.enter_context(nc.sbuf_tensor("hnlo", [128, 8, 128], BF16))
                rpb = ph2.enter_context(nc.sbuf_tensor("rpb", [128, 8], F32))
                rp2b = ph2.enter_context(nc.sbuf_tensor("rp2b", [128, 8], F32))

                for i in range(8):
                    ch_pe.wait(nc.sync)
                    ld_sync(tmp32[:, :], opw_in.ap()[128 * i:128 * (i + 1), :])
                    ch_in.wait(nc.tensor)
                    V.wait(nc.tensor)
                    for j in range(8):
                        ins = nc.tensor.transpose(
                            ps_C[:, 512 * (j % 2):512 * (j % 2) + 128],
                            tmp32[:, 128 * j:128 * (j + 1)], identf[:, :])
                        ch_pe.inc(ins)
                        ch_pe.wait(nc.vector)
                        V.tensor_copy(out=opjT[:, j, 128 * i:128 * (i + 1)],
                                      in_=ps_C[:, 512 * (j % 2):512 * (j % 2) + 128])
                        V.wait(nc.tensor)
                ld_sync(ipbrow[0:1, 0:E], opb_in.ap()[None, :])
                ch_in.wait(nc.vector)
                V.tensor_copy(out=opbr[0:1, :], in_=ipbrow[0:1, 0:E])
                V.wait(nc.sync)
                rep = 1
                while rep < 128:
                    ch_st.pre(nc.sync)
                    ch_st.dma(nc.sync.dma_start(
                        out=opbr[rep:2 * rep, :], in_=opbr[0:rep, :]))
                    rep *= 2
                ch_st.wait(nc.vector)
                ld_sync(rw8[:, :], rw_in.ap())
                ch_in.wait(nc.tensor)
                for c in range(8):
                    ch_pe.wait(nc.vector)
                    ins = nc.tensor.transpose(
                        ps_E[:, 0:8], rw8[:, 128 * c:128 * (c + 1)],
                        identf[0:8, 0:8])
                    ch_pe.inc(ins)
                    ch_pe.wait(nc.vector)
                    V.tensor_copy(out=rwT32[:, c, :], in_=ps_E[:, 0:8])
                    V.wait(nc.tensor)
                V.tensor_copy(out=rwThi[:, :, :], in_=rwT32[:, :, :])
                V.tensor_copy(out=rwThi32[:, :, :], in_=rwThi[:, :, :])
                V.tensor_tensor(out=rwTlo[:, :, :], in0=rwT32[:, :, :],
                                in1=rwThi32[:, :, :], op=ALU.subtract)

                pm_flat = pmb[:, :, :].rearrange("p n t -> p (n t)")
                pt_flat = ptb[:, :, :].rearrange("p n t -> p (n t)")
                qib_s = [qib, ph2.enter_context(nc.sbuf_tensor("qib_b", [128, E], BF16))]
                kib_s = [kib, ph2.enter_context(nc.sbuf_tensor("kib_b", [128, E], BF16))]
                vib_s = [vib, ph2.enter_context(nc.sbuf_tensor("vib_b", [128, E], BF16))]
                xng_s = [xng, ph2.enter_context(nc.sbuf_tensor("xng_b", [128, E], F32))]
                g_ready = [None, None]
                g_cons_v = [0, 0]
                g_cons_pe = [0, 0]

                def ld_grp(g2):
                    slot = g2 % 2
                    nc.sync.wait_ge(V.sem, g_cons_v[slot])
                    nc.sync.wait_ge(ch_pe.sem, g_cons_pe[slot])
                    for buf, qidx in ((qib_s[slot], 0), (kib_s[slot], 1),
                                      (vib_s[slot], 2)):
                        ch_in.pre(nc.sync)
                        ch_in.dma(nc.sync.dma_start(
                            out=buf[:, :],
                            in_=qkv_dr.ap()[4 * g2:4 * g2 + 4, :, qidx, :, :]
                            .rearrange("a b h d -> (a b) (h d)")))
                    ch_in.pre(nc.sync)
                    ch_in.dma(nc.sync.dma_start(
                        out=xng_s[slot][:, :],
                        in_=xn_dr.ap()[4 * g2:4 * g2 + 4, :, :]
                        .rearrange("a b e -> (a b) e")))
                    g_ready[slot] = ch_in.n

                ld_grp(0)
                for g in range(64):
                    gslot = g % 2
                    if g + 1 < 64:
                        ld_grp(g + 1)
                    qib, kib, vib, xng = (qib_s[gslot], kib_s[gslot],
                                          vib_s[gslot], xng_s[gslot])
                    ch_st.wait(nc.vector)
                    ch_st.wait(nc.scalar)
                    nc.vector.wait_ge(ch_in.sem, g_ready[gslot])
                    V.transpose(out=qTt[:, :], in_=qib[:, :])
                    V.transpose(out=kTt[:, :], in_=kib[:, :])
                    V.wait(nc.tensor)
                    A.wait(nc.tensor)
                    for b in range(4):
                        for n in range(H):
                            for hf in range(2):
                                ins = nc.tensor.matmul(
                                    ps_D[32 * b:32 * b + 32,
                                         32 * n:32 * n + 32],
                                    qTt[32 * b:32 * b + 32,
                                        64 * n + 32 * hf:64 * n + 32 * hf + 32],
                                    kTt[32 * b:32 * b + 32,
                                        64 * n + 32 * hf:64 * n + 32 * hf + 32],
                                    start=(hf == 0), stop=(hf == 1),
                                    tile_position=(32 * b, 32 * b),
                                    skip_group_check=True)
                        ch_pe.inc(ins)
                    ch_pe.wait(nc.scalar)
                    A.activation(out=exb[:, :, :],
                                 in_=ps_D[:, :].rearrange(
                                     "p (n t) -> p n t", t=32),
                                 func=AF.Exp, scale=0.125)
                    A.wait(nc.vector)
                    V.tensor_tensor(out=pmb[:, :, :], in0=exb[:, :, :],
                                    in1=mask[:, :, :], op=ALU.mult)
                    V.tensor_reduce(out=sums[:, :], in_=pmb[:, :, :],
                                    axis=AX.X, op=ALU.add)
                    V.reciprocal(out=recs[:, :], in_=sums[:, :])
                    V.transpose(out=pt_flat, in_=pm_flat)
                    V.wait(nc.tensor)
                    for b in range(4):
                        for n in range(H):
                            ins = nc.tensor.matmul(
                                ps_A[32 * b:32 * b + 32, 64 * n:64 * n + 64],
                                ptb[32 * b:32 * b + 32, n, :],
                                vib[32 * b:32 * b + 32, 64 * n:64 * n + 64],
                                start=True, stop=True,
                                tile_position=(32 * b, 32 * b),
                                skip_group_check=True)
                        ch_pe.inc(ins)
                    ch_pe.wait(nc.vector)
                    for n in range(H):
                        V.tensor_scalar(out=at32[:, 64 * n:64 * n + 64],
                                        in0=ps_A[:, 64 * n:64 * n + 64],
                                        scalar1=recs[:, n:n + 1],
                                        scalar2=None, op0=ALU.mult)
                    transpose_f32(atT, at32, 8, ps_B)
                    for half in range(2):
                        for c in range(8):
                            ins = nc.tensor.matmul(
                                ps_C[:, 512 * half:512 * half + 512],
                                atT[:, c, :],
                                opjT[:, c, 512 * half:512 * half + 512],
                                start=(c == 0), stop=(c == 7))
                        ch_pe.inc(ins)
                    ch_pe.wait(nc.vector)
                    V.tensor_tensor(out=x1sb[:, :], in0=ps_C[:, :],
                                    in1=opbr[:, :], op=ALU.add)
                    V.tensor_tensor(out=x1sb[:, :], in0=x1sb[:, :],
                                    in1=xng[:, :], op=ALU.add)
                    V.wait(nc.sync)
                    st_sync(x1_dr.ap()[4 * g:4 * g + 4, :, :]
                            .rearrange("a b e -> (a b) e"), x1sb[:, :])
                    ln_stats(nc, V, A, x1sb[:, :], mu, rr, work, junk)
                    V.tensor_scalar(out=z32[:, :], in0=x1sb[:, :],
                                    scalar1=mu[:, 0:1], scalar2=rr[:, 0:1],
                                    op0=ALU.subtract, op1=ALU.mult)
                    V.wait(nc.tensor)
                    A.wait(nc.tensor)
                    for c in range(8):
                        ins = nc.tensor.transpose(
                            ps_B[:, 128 * c:128 * (c + 1)],
                            z32[:, 128 * c:128 * (c + 1)], identf[:, :])
                    ch_pe.inc(ins)
                    ch_pe.wait(nc.vector)
                    for c in range(8):
                        V.tensor_scalar(out=hnT32[:, c, :],
                                        in0=ps_B[:, 128 * c:128 * (c + 1)],
                                        scalar1=ln2w[:, c:c + 1],
                                        scalar2=ln2b[:, c:c + 1],
                                        op0=ALU.mult, op1=ALU.add)
                    V.wait(nc.tensor)
                    V.tensor_copy(out=hnbf[:, :, :], in_=hnT32[:, :, :])
                    V.tensor_tensor(out=hnlo[:, :, :], in0=hnT32[:, :, :],
                                    in1=hnbf[:, :, :], op=ALU.subtract)
                    V.wait(nc.sync)
                    st_sync(hn2_dr.ap()[g // 4, :, :,
                                        128 * (g % 4):128 * (g % 4) + 128]
                            .rearrange("c p f -> p c f"),
                            hnbf[:, :, :])
                    V.wait(nc.tensor)
                    first = True
                    for lh, rh in ((hnbf, rwThi), (hnbf, rwTlo),
                                   (hnlo, rwThi)):
                        for c in range(8):
                            ins = nc.tensor.matmul(
                                ps_E[:, 0:8], lh[:, c, :], rh[:, c, :],
                                start=first, stop=(lh is hnlo and c == 7))
                            first = False
                    ch_pe.inc(ins)
                    ch_pe.wait(nc.vector)
                    V.tensor_reduce(out=work[:, 11:12], in_=ps_E[:, 0:8],
                                    axis=AX.X, op=ALU.max, negate=True)
                    V.wait(nc.scalar)
                    A.activation(out=rpb[:, :], in_=ps_E[:, 0:8], func=AF.Exp,
                                 bias=work[:, 11:12],
                                 accum_out=work[:, 12:13])
                    A.wait(nc.vector)
                    V.reciprocal(out=work[:, 13:14], in_=work[:, 12:13])
                    V.tensor_scalar(out=rpb[:, :], in0=rpb[:, :],
                                    scalar1=work[:, 13:14], scalar2=None,
                                    op0=ALU.mult)
                    V.tensor_reduce(out=work[:, 14:15], in_=rpb[:, :],
                                    axis=AX.X, op=ALU.max)
                    V.scalar_tensor_tensor(out=rp2b[:, :], in0=rpb[:, :],
                                           scalar=work[:, 14:15],
                                           in1=rpb[:, :],
                                           op0=ALU.is_lt, op1=ALU.mult)
                    V.tensor_reduce(out=work[:, 15:16], in_=rp2b[:, :],
                                    axis=AX.X, op=ALU.max)
                    V.scalar_tensor_tensor(out=gates[:, 8 * g:8 * g + 8],
                                           in0=rpb[:, :],
                                           scalar=work[:, 15:16],
                                           in1=rpb[:, :],
                                           op0=ALU.is_ge, op1=ALU.mult)
                    g_cons_v[gslot] = V.n
                    g_cons_pe[gslot] = ch_pe.n

            if stage >= 2:
                fence()
                ph2.close()
            if stage >= 3:
                ph3 = ExitStack()
                w1b = [ph3.enter_context(nc.sbuf_tensor(f"w1b{i}", [128, 8, FF], BF16))
                       for i in range(2)]
                w2b = [ph3.enter_context(nc.sbuf_tensor(f"w2b{i}", [128, 16, E], BF16))
                       for i in range(2)]
                hb = ph3.enter_context(nc.sbuf_tensor("hb", [128, 8, 512], BF16))
                hmidT = ph3.enter_context(nc.sbuf_tensor("hmidT", [128, 16, 512], BF16))
                ysb = ph3.enter_context(nc.sbuf_tensor("ysb", [128, E], F32))
                ysb2 = ph3.enter_context(nc.sbuf_tensor("ysb2", [128, E], F32))

                def ld_expert(par, e):
                    ch_ing.pre(nc.gpsimd)
                    ch_ing.dma(nc.gpsimd.dma_start(
                        out=w1b[par][:, :, :],
                        in_=w1_in.ap()[e].rearrange("(c p) f -> p c f",
                                                    p=128)))
                    ch_ing.pre(nc.gpsimd)
                    ch_ing.dma(nc.gpsimd.dma_start(
                        out=w2b[par][:, :, :],
                        in_=w2_in.ap()[e].rearrange("(c p) f -> p c f",
                                                    p=128)))

                ld_expert(0, 0)
                hb2 = [hb, ph3.enter_context(nc.sbuf_tensor("hb_b", [128, 8, 512], BF16))]
                ysbs = [ysb, ph3.enter_context(nc.sbuf_tensor("ysb_b", [128, E], F32))]
                psDE = (ps_D, ps_E)
                psAB = (ps_A, ps_B)
                hb_users = [0, 0]
                hb_ready = [None, None]
                y_ready = {}
                y_store_cnt = {}
                yld_users = [0, 0]

                def ld_hb(sg2):
                    slot = sg2 % 2
                    nc.sync.wait_ge(ch_pe.sem, hb_users[slot])
                    ch_in.pre(nc.sync)
                    ch_in.dma(nc.sync.dma_start(
                        out=hb2[slot][:, :, :],
                        in_=hn2_dr.ap()[sg2].rearrange("c p f -> p c f")))
                    hb_ready[slot] = ch_in.n

                def ld_y(e2, sg2, tc2):
                    slot = tc2 % 2
                    nc.sync.wait_ge(V.sem, yld_users[slot])
                    prior = y_store_cnt.get((sg2, tc2))
                    if prior is not None:
                        nc.sync.wait_ge(ch_st.sem, prior)
                    ch_in.pre(nc.sync)
                    ch_in.dma(nc.sync.dma_start(
                        out=ysbs[slot][:, :],
                        in_=y_dr.ap()[512 * sg2 + 128 * tc2:
                                      512 * sg2 + 128 * tc2 + 128, :]))
                    y_ready[(e2, sg2, tc2)] = ch_in.n

                ld_hb(0)
                for e in range(NEXP):
                    par = e % 2
                    wready = ch_ing.n
                    if e + 1 < NEXP:
                        nc.gpsimd.wait_ge(ch_pe.sem, ch_pe.n)
                        ld_expert(1 - par, e + 1)
                    nc.tensor.wait_ge(ch_ing.sem, wready)
                    for sg in range(16):
                        hslot = sg % 2
                        if (sg + 1 < 16) or (e + 1 < NEXP):
                            ld_hb((sg + 1) % 16)
                        nc.tensor.wait_ge(ch_in.sem, hb_ready[hslot])
                        if e > 0:
                            ld_y(e, sg, 0)
                        gelu_done = []
                        for fc in range(16):
                            pd = psDE[fc % 2]
                            if fc >= 2:
                                nc.tensor.wait_ge(A.sem, gelu_done[fc - 2])
                            for kc in range(8):
                                ins = nc.tensor.matmul(
                                    pd[:, :],
                                    w1b[par][:, kc, 128 * fc:128 * (fc + 1)],
                                    hb2[hslot][:, kc, :],
                                    start=(kc == 0), stop=(kc == 7))
                            ch_pe.inc(ins)
                            ch_pe.wait(nc.scalar)
                            A.activation(out=hmidT[:, fc, :], in_=pd[:, :],
                                         func=AF.Gelu)
                            gelu_done.append(A.n)
                        hb_users[hslot] = ch_pe.n
                        nc.tensor.wait_ge(A.sem, gelu_done[15])
                        stt_done = []
                        for tc in range(4):
                            pa = psAB[tc % 2]
                            gcol = 8 * (4 * sg + tc) + e
                            row0 = 512 * sg + 128 * tc
                            if e > 0 and tc + 1 < 4:
                                ld_y(e, sg, tc + 1)
                            if tc >= 2:
                                nc.tensor.wait_ge(V.sem, stt_done[tc - 2])
                            for fc in range(16):
                                for half in range(2):
                                    ins = nc.tensor.matmul(
                                        pa[:, 512 * half:512 * half + 512],
                                        hmidT[:, fc, 128 * tc:128 * (tc + 1)],
                                        w2b[par][:, fc,
                                                 512 * half:512 * half + 512],
                                        start=(fc == 0), stop=(fc == 15))
                            ch_pe.inc(ins)
                            ch_pe.wait(nc.vector)
                            if e > 0:
                                nc.vector.wait_ge(ch_in.sem,
                                                  y_ready[(e, sg, tc)])
                                V.scalar_tensor_tensor(
                                    out=ysb2[:, :], in0=pa[:, :],
                                    scalar=gates[:, gcol:gcol + 1],
                                    in1=ysbs[tc % 2][:, :],
                                    op0=ALU.mult, op1=ALU.add)
                                yld_users[tc % 2] = V.n
                            else:
                                V.tensor_scalar(
                                    out=ysb2[:, :], in0=pa[:, :],
                                    scalar1=gates[:, gcol:gcol + 1],
                                    scalar2=None, op0=ALU.mult)
                            stt_done.append(V.n)
                            V.wait(nc.sync)
                            st_sync(y_dr.ap()[row0:row0 + 128, :], ysb2[:, :])
                            y_store_cnt[(sg, tc)] = ch_st.n
                fence()
                ph3.close()

            if stage >= 4:
                ph4 = ExitStack()
                wo_nat = ph4.enter_context(nc.sbuf_tensor("wonat", [128, 8, E], BF16))
                kbufs = [ph4.enter_context(nc.sbuf_tensor(f"kbuf{i}", [65, 8, T], BF16)) for i in range(2)]
                vbufs = [ph4.enter_context(nc.sbuf_tensor(f"vbuf{i}", [128, 16, 8, 65], BF16)) for i in range(2)]
                qbuf = ph4.enter_context(nc.sbuf_tensor("qbuf", [65, H, SLOC], BF16))
                pTs = [ph4.enter_context(nc.sbuf_tensor(f"pts{i}", [128, 16, SLOC], BF16)) for i in range(2)]
                a32 = ph4.enter_context(nc.sbuf_tensor("a32", [128, 2, E], F32))
                atT2 = ph4.enter_context(nc.sbuf_tensor("atT2", [128, 8, 128], BF16))
                recb = ph4.enter_context(nc.sbuf_tensor("recb", [128, 2], F32))
                xg = ph4.enter_context(nc.sbuf_tensor("xg", [128, E], F32))
                x1g = ph4.enter_context(nc.sbuf_tensor("x1g", [128, E], F32))
                yg = ph4.enter_context(nc.sbuf_tensor("yg", [128, E], F32))
                outsb = ph4.enter_context(nc.sbuf_tensor("outsb", [128, E], F32))

                ch_ing.pre(nc.gpsimd)
                ch_ing.dma(nc.gpsimd.dma_start(
                    out=wo_nat[:, :, :],
                    in_=wo_in.ap().rearrange("(c p) d -> p c d", p=128)))
                for i in range(2):
                    V.memset(vbufs[i][:, :, :, 64:65], 1.0)
                nc.tensor.wait_ge(ch_ing.sem, ch_ing.n)
                psem = nc.alloc_semaphore("pollsem")
                V.wait(nc.gpsimd)
                ch_stg.wait(nc.gpsimd)
                ch_cp.wait(nc.gpsimd)
                with nc.gpsimd.register("fv") as freg, \
                     nc.gpsimd.register("f2") as freg2, \
                     nc.gpsimd.register("pk") as kreg:
                    nc.gpsimd.reg_mov(freg, 0)
                    nc.gpsimd.reg_mov(freg2, 0)
                    nc.gpsimd.reg_mov(kreg, 0)

                    def _cond():
                        a = nc.gpsimd.snap(freg, min_val=0, max_val=10,
                                           donate=True)
                        b2 = nc.gpsimd.snap(freg2, min_val=0, max_val=10,
                                            donate=True)
                        return a + b2 < 2
                    with nc.gpsimd.While(_cond):
                        nc.gpsimd.reg_alu(
                            kreg, nc.gpsimd.snap(kreg, min_val=0,
                                                 max_val=10**9), 16, ALU.add)
                        nc.gpsimd.dma_start(
                            out=fbz[0:2, :], in_=flag_sh.ap()[:, :]
                        ).then_inc(psem, 16)
                        nc.gpsimd.wait_ge(
                            psem, nc.gpsimd.snap(kreg, min_val=0,
                                                 max_val=10**9))
                        nc.reg_load(freg, fbz[0:1, 0:1])
                        nc.reg_load(freg2, fbz[1:2, 0:1])
                items = [(b_o, hf) for b_o in range(B) for hf in range(2)]
                kv_ready = []

                def ld_kv(i):
                    b_o, hf = items[i]
                    slot = i % 2
                    nc.gpsimd.wait_ge(ch_pe.sem, ch_pe.n)
                    V.wait(nc.gpsimd)
                    ch_ing.pre(nc.gpsimd)
                    ch_ing.dma(nc.gpsimd.dma_start(
                        out=kbufs[slot][:, :, :],
                        in_=kct_sh.ap()[b_o, 8 * hf:8 * hf + 8, :, :]
                        .rearrange("h d t -> d h t")))
                    for hh8 in range(8):
                        ch_ing.pre(nc.gpsimd)
                        ch_ing.dma(nc.gpsimd.dma_start(
                            out=vbufs[slot][:, :, hh8, 0:64],
                            in_=v_sh.ap()[b_o, 8 * hf + hh8, :, :]
                            .rearrange("(c p) d -> p c d", p=128)))
                    kv_ready.append(ch_ing.n)

                ld_kv(0)
                exp_on_slot = [0, 0]
                psAB2 = (ps_A, ps_B)
                qbufs = [qbuf, ph4.enter_context(nc.sbuf_tensor("qbuf_b", [65, H, SLOC], BF16))]
                q_ready = [None, None]
                q_users = [0, 0]

                def ld_q(b_o2):
                    slot = b_o2 % 2
                    nc.sync.wait_ge(ch_pe.sem, q_users[slot])
                    ch_in.pre(nc.sync)
                    ch_in.dma(nc.sync.dma_start(
                        out=qbufs[slot][:, :, :],
                        in_=qt_dr.ap()[b_o2].rearrange("h d t -> d h t")))
                    q_ready[slot] = ch_in.n

                ld_q(0)
                v_evac = 0
                for i, (b_o, hf) in enumerate(items):
                    slot = i % 2
                    qslot = b_o % 2
                    if i + 1 < len(items):
                        ld_kv(i + 1)
                    if hf == 0 and b_o + 1 < B:
                        ld_q(b_o + 1)
                    nc.tensor.wait_ge(ch_in.sem, q_ready[qslot])
                    nc.tensor.wait_ge(ch_ing.sem, kv_ready[i])
                    for hh in range(8):
                        h = 8 * hf + hh
                        hp = h % 2
                        for grp in range(4):
                            pab = psAB2[grp % 2]
                            nc.tensor.wait_ge(A.sem, exp_on_slot[grp % 2])
                            for tq in range(4):
                                tc = 4 * grp + tq
                                ins = nc.tensor.matmul(
                                    pab[:, 256 * tq:256 * (tq + 1)],
                                    kbufs[slot][:, hh, 128 * tc:128 * (tc + 1)],
                                    qbufs[qslot][:, h, :],
                                    start=True, stop=True)
                            ch_pe.inc(ins)
                            ch_pe.wait(nc.scalar)
                            A.activation(
                                out=pTs[hp][:, 4 * grp:4 * grp + 4, :],
                                in_=pab[:, :].rearrange("p (c f) -> p c f",
                                                        f=256),
                                func=AF.Exp, scale=1.0 / 64.0)
                            exp_on_slot[grp % 2] = A.n
                        a_pv = A.n
                        nc.tensor.wait_ge(A.sem, a_pv)
                        nc.tensor.wait_ge(V.sem, v_evac)
                        for sc in range(2):
                            for tc in range(16):
                                ins = nc.tensor.matmul(
                                    ps_D[:, 65 * sc:65 * sc + 65],
                                    pTs[hp][:, tc, 128 * sc:128 * sc + 128],
                                    vbufs[slot][:, tc, hh, :],
                                    start=(tc == 0), stop=(tc == 15))
                            ch_pe.inc(ins)
                        ch_pe.wait(nc.vector)
                        for sc in range(2):
                            V.reciprocal(out=recb[:, sc:sc + 1],
                                         in_=ps_D[:, 65 * sc + 64:65 * sc + 65])
                            V.tensor_scalar(
                                out=a32[:, sc, 64 * h:64 * h + 64],
                                in0=ps_D[:, 65 * sc:65 * sc + 64],
                                scalar1=recb[:, sc:sc + 1], scalar2=None,
                                op0=ALU.mult)
                        v_evac = V.n
                    q_users[qslot] = ch_pe.n
                    if hf == 1:
                        for sc in range(2):
                            s0 = 128 * sc
                            V.wait(nc.tensor)
                            A.wait(nc.tensor)
                            for rnd in range(2):
                                for c4 in range(4):
                                    c = 4 * rnd + c4
                                    ins = nc.tensor.transpose(
                                        ps_E[:, 128 * c4:128 * (c4 + 1)],
                                        a32[:, sc, 128 * c:128 * (c + 1)],
                                        identf[:, :])
                                ch_pe.inc(ins)
                                ch_pe.wait(nc.vector)
                                V.tensor_copy(
                                    out=atT2[:, 4 * rnd:4 * rnd + 4, :],
                                    in_=ps_E[:, :].rearrange(
                                        "p (c f) -> p c f", f=128))
                                V.wait(nc.tensor)
                            for half in range(2):
                                for c in range(8):
                                    ins = nc.tensor.matmul(
                                        ps_C[:, 512 * half:512 * half + 512],
                                        atT2[:, c, :],
                                        wo_nat[:, c, 512 * half:512 * half + 512],
                                        start=(c == 0), stop=(c == 7))
                                ch_pe.inc(ins)
                            V.wait(nc.sync)
                            ch_in.pre(nc.sync)
                            ch_in.dma(nc.sync.dma_start(
                                out=xg[:, :],
                                in_=x_in.ap()[b_o, s0:s0 + 128, :]))
                            ch_in.pre(nc.sync)
                            ch_in.dma(nc.sync.dma_start(
                                out=x1g[:, :],
                                in_=x1_dr.ap()[s0:s0 + 128, b_o, :]))
                            ch_in.pre(nc.sync)
                            ch_in.dma(nc.sync.dma_start(
                                out=yg[:, :],
                                in_=y_dr.ap().rearrange(
                                    "(a b) e -> a b e", b=B)
                                [s0:s0 + 128, b_o, :]))
                            ch_in.wait(nc.vector)
                            ch_pe.wait(nc.vector)
                            ch_res.wait(nc.vector)
                            V.tensor_tensor(out=outsb[:, :], in0=ps_C[:, :],
                                            in1=xg[:, :], op=ALU.add)
                            V.tensor_tensor(out=outsb[:, :], in0=outsb[:, :],
                                            in1=x1g[:, :], op=ALU.add)
                            V.tensor_tensor(out=outsb[:, :], in0=outsb[:, :],
                                            in1=yg[:, :], op=ALU.add)
                            V.wait(nc.sync)
                            ch_res.pre(nc.sync)
                            ch_res.dma(nc.sync.dma_start(
                                out=out_dr.ap()[b_o, s0:s0 + 128, :],
                                in_=outsb[:, :]))
                ch_res.wait(nc.sync)
                fence()
                ph4.close()
            ALLCH.append(ch_cp)
            fence()
    return nc
